# version 01 (final)
# baseline (speedup 1.0000x reference)
import numpy as np
import ml_dtypes
import concourse.bass as bass
import concourse.mybir as mybir
from concourse.bass_utils import run_bass_kernel_spmd

F32 = mybir.dt.float32
BF16 = mybir.dt.bfloat16
I32 = mybir.dt.int32
AF = mybir.ActivationFunctionType
ALU = mybir.AluOpType
AX = mybir.AxisListType
BF16NP = ml_dtypes.bfloat16

S, D, NC, TOK = 8192, 2048, 8, 1024
NCH = 16
H0, DN, DR, DV, QLR, KVLR = 16, 128, 64, 128, 1536, 512
H1, HD1 = 32, 64
SC0 = 1.0 / float(np.sqrt(DN + DR))
SC1 = 1.0 / float(np.sqrt(HD1))
NEG = -1e38
EPS_RMS = 1e-6
EPS_LN = 1e-5

ENGS = ("sync", "scalar", "vector", "tensor", "gpsimd")


def own_chunks(c):
    return (c, NCH - 1 - c)


def chunk_rank(C):
    return C if C < 8 else NCH - 1 - C


def chunk_half(C):
    return 0 if C < 8 else 1


class Prog:
    def __init__(self, nc):
        self.nc = nc
        self.ops = {e: [] for e in ENGS}
        self.sems = {}
        self.count = {}

    def sem(self, name):
        if name not in self.sems:
            self.sems[name] = self.nc.alloc_semaphore(name)
            self.count[name] = 0
        return name

    def do(self, eng, fn, waits=(), inc=None, amt=1):
        wd = {}
        for w in waits:
            if w is None:
                continue
            sem, val, peng, isdma = w
            assert val <= self.count.get(sem, 0), f"wait on future event {w}"
            if peng == eng == "tensor" and not isdma:
                continue
            wd[sem] = max(wd.get(sem, 0), val)
        ws = list(wd.items())
        ev = None
        if inc is not None:
            self.sem(inc)
            self.count[inc] += amt
            ev = (inc, self.count[inc], eng, amt == 16)
        self.ops[eng].append((fn, ws, inc, amt if inc else 0))
        return ev

    def dma(self, eng, out, in_, waits=(), inc=None, **kw):
        return self.do(eng, lambda e: e.dma_start(out, in_, **kw),
                       waits=waits, inc=inc, amt=16)

    def emit(self):
        nc = self.nc
        with nc.Block() as block:
            for ename in ENGS:
                ops = self.ops[ename]
                if not ops:
                    continue

                def make_body(ops, prog):
                    def body(eng):
                        seen = {}
                        for fn, ws, sem, amt in ops:
                            for s, v in ws:
                                if seen.get(s, -1) >= v:
                                    continue
                                eng.wait_ge(prog.sems[s], v)
                                seen[s] = v
                            ins = fn(eng)
                            if sem is not None:
                                assert ins is not None, "op with inc returned None"
                                ins.then_inc(prog.sems[sem], amt)
                    return body

                getattr(block, ename)(make_body(list(ops), self))


TGS, TFW = 0, 8


def host_tables(c):
    gA, gB = own_chunks(c)
    nA = (4 * gA + 7) // 8
    gs, mcol = [], np.zeros((64,), np.float32)
    for s in range(8):
        g = gA if s < nA else gB
        sl = s if s < nA else s - nA
        for hf2 in range(2):
            kcb = sl * 8 + hf2 * 4
            if kcb >= 4 * g:
                mcol[s * 8 + hf2 * 4: s * 8 + hf2 * 4 + 4] = NEG
        gs.append(0 if s < nA else 512)
    tab = np.array(gs + [gA * 512, gB * 512], np.int32)
    tab = np.pad(tab, (0, 96 - tab.size))
    mcolf = np.ascontiguousarray(np.broadcast_to(mcol[None, :], (128, 64))).astype(np.float32)
    return tab.reshape(96, 1), mcolf


def host_idx(c):
    gA, gB = own_chunks(c)
    nA = (4 * gA + 7) // 8
    ix = np.zeros((128, 128), np.int32)
    p = np.arange(128)

    def half_src(j):
        s, hf2 = j // 2, j % 2
        g = gA if s < nA else gB
        sl = s if s < nA else s - nA
        kcb = sl * 8 + hf2 * 4
        if kcb >= 4 * g:
            return 8, 0
        Ct = kcb // 4
        return chunk_rank(Ct), chunk_half(Ct)

    for j in range(16):
        r, hf = half_src(j)
        ix[:, j] = (r * 2240 + p) * 2 + hf
        ix[:, 18 + j] = (r * 64 + np.minimum(p, 63)) * 2 + hf
    for gi, g in enumerate((gA, gB)):
        r, hf = chunk_rank(g), chunk_half(g)
        ix[:, 16 + gi] = (r * 2240 + p) * 2 + hf
        ix[:, 34 + gi] = (r * 64 + np.minimum(p, 63)) * 2 + hf
    for j in range(64):
        s, i = j // 8, j % 8
        r, hf = half_src(s * 2 + i // 4)
        ix[:, 36 + j] = r * 1024 + hf * 512 + (i % 4) * 128 + p
    for gi, g in enumerate((gA, gB)):
        r, hf = chunk_rank(g), chunk_half(g)
        for w in range(4):
            ix[:, 100 + gi * 4 + w] = r * 1024 + hf * 512 + w * 128 + p
    for s in range(8):
        gsl = 0 if s < nA else 1
        ix[:, 108 + s] = p + gsl * 128
        ix[0, 118 + s] = gsl
        ix[1, 118 + s] = 1 - gsl
    ix[:, 116] = p
    ix[:, 117] = p + 128
    ix[0, 126], ix[1, 126] = 0, 1
    ix[0, 127], ix[1, 127] = 1, 0
    return ix


def host_consts():
    ident_f = np.eye(128, dtype=np.float32)
    ident_b = ident_f.astype(BF16NP)
    masks = np.zeros((4, 128, 512), np.float32)
    k = np.arange(128)[:, None]
    q = np.arange(512)[None, :]
    for j in range(4):
        masks[j] = np.where(q >= j * 128 + k, 0.0, NEG)
    sel0 = np.zeros((17, 128, 16), np.float32)
    for h in range(16):
        sel0[h, :, h] = 1.0
    sel0[16, 0:64, :] = 1.0
    sel1 = np.zeros((16, 128, 32), np.float32)
    for m in range(16):
        sel1[m, 0:64, 2 * m] = 1.0
        sel1[m, 64:128, 2 * m + 1] = 1.0
    return dict(identb=ident_b, identf=ident_f,
                masks=masks.reshape(4 * 128, 512),
                sel0=sel0.astype(BF16NP).reshape(17 * 128, 16),
                sel1=sel1.astype(BF16NP).reshape(16 * 128, 32))


def host_cossin(c):
    gA, gB = own_chunks(c)
    pos = np.concatenate([np.arange(gA * 512, (gA + 1) * 512),
                          np.arange(gB * 512, (gB + 1) * 512)]).astype(np.float64)
    inv = 1.0 / (10000.0 ** (np.arange(0, DR, 2) / DR))
    ang = pos[None, :] * inv[:, None]
    cs = np.zeros((64, 1024), np.float32)
    cs[0:32] = np.cos(ang)
    cs[32:64] = np.sin(ang)
    return cs


class MK:
    def __init__(self, dbg=()):
        self.dbg = set(dbg)
        nc = bass.Bass("TRN2", target_bir_lowering=False)
        self.nc = nc
        self.P = Prog(nc)
        d = nc.dram_tensor

        self.x_in = d("x_in", [TOK, D], F32, kind="ExternalInput")
        shapes = dict(L0_attn_wq_a=[D, QLR],
                      L0_attn_wq_b=[QLR, 3072], L0_attn_wkv_a=[D, 576],
                      L0_attn_wkv_b=[KVLR, 4096],
                      L0_attn_wo=[D, D],
                      L0_ffn_w_gate=[D, 8192], L0_ffn_w_up=[D, 8192],
                      L0_ffn_w_down=[8192, D],
                      L1_attn_Wq=[D, D], L1_attn_Wk=[D, D], L1_attn_Wv=[D, D],
                      L1_attn_Wf=[H1, D], L1_attn_bf=[H1, 1], L1_attn_Wo=[D, D],
                      L1_ffn_w_gate=[D, 8192], L1_ffn_w_up=[D, 8192],
                      L1_ffn_w_down=[8192, D])
        self.w = {n: d(n, sh, F32, kind="ExternalInput") for n, sh in shapes.items()}
        self.gq_in = d("gq_cols", [128, 12], F32, kind="ExternalInput")
        self.gkv_in = d("gkv_cols", [128, 4], F32, kind="ExternalInput")
        self.tab_in = d("tab", [96, 1], I32, kind="ExternalInput")
        self.idx_in = d("sidx", [128, 128], I32, kind="ExternalInput")
        self.mcol_in = d("mcol", [128, 64], F32, kind="ExternalInput")
        self.masks_in = d("masks", [4 * 128, 512], F32, kind="ExternalInput")
        self.identb_in = d("identb", [128, 128], BF16, kind="ExternalInput")
        self.identf_in = d("identf", [128, 128], F32, kind="ExternalInput")
        self.sel0_in = d("sel0", [17 * 128, 16], BF16, kind="ExternalInput")
        self.sel1_in = d("sel1", [16 * 128, 32], BF16, kind="ExternalInput")
        self.cs_in = d("cs", [64, 1024], F32, kind="ExternalInput")
        self.y_out = d("y_out", [TOK, D], F32, kind="ExternalOutput")

        self.d_x1 = d("d_x1", [TOK, D], F32)
        self.d_q = d("d_q", [3088, TOK], BF16)
        self.d_o = d("d_o", [D, TOK], BF16)
        self.d_g = d("d_g", [D, TOK], BF16)
        self.d_kin = d("d_kin", [2240, 1024], BF16)
        self.d_vin = d("d_vin", [1024, 2048], BF16)
        self.d_rin = d("d_rin", [64, 1024], BF16)
        self.d_fin = d("d_fin", [32, 1024], F32)
        self.d_nin = d("d_nin", [32, 1], F32)
        self.d_kout = d("d_kout", [9 * 2240, 1024], BF16)
        self.d_vout = d("d_vout", [9 * 1024, 2048], BF16)
        self.d_rout = d("d_rout", [9 * 64, 1024], BF16)
        self.d_fout = d("d_fout", [8 * 32, 1024], F32)
        self.d_nout = d("d_nout", [32, 1], F32)
        self.d_wbf = [d(f"d_wbf{i}", [D, D], BF16) for i in range(4)]
        self.d_vord = d("d_vord", [9216, 2048], BF16)
        self.d_qslots = d("d_qslots", [3088, 4096], BF16)
        self.d_oacc = d("d_oacc", [256, 512], F32)
        self.d_sg = d("d_sg", [TOK, D], BF16)
        self.d_F = d("d_F", [32, 8192], F32)
        self.d_zacc = d("d_zacc", [2, 512], F32)
        self.dbg_outs = {}

        a = nc.alloc_sbuf_tensor
        self.s_xn = a("s_xn", [128, 16 * 1024], BF16)
        self.s_work = a("s_work", [128, 32768], BF16)
        self.s_aux = a("s_aux", [128, 6144], BF16)
        self.s_wst = a("s_wst", [128, 16384], BF16)
        self.s_f32 = a("s_f32", [128, 10240], F32)
        self.s_stat = a("s_stat", [128, 2048], F32)
        self.s_masks = a("s_masks", [128, 4 * 512], F32)
        self.s_cs = a("s_cs", [64, 1024], F32)
        self.s_identb = a("s_identb", [128, 128], BF16)
        self.s_identf = a("s_identf", [128, 128], F32)
        self.s_sel0 = a("s_sel0", [128, 17 * 16], BF16)
        self.s_sel1 = a("s_sel1", [128, 16 * 32], BF16)
        self.s_gain = a("s_gain", [128, 16], F32)
        self.s_mcol = a("s_mcol", [128, 64], F32)
        self.s_tab = a("s_tab", [1, 96], I32)
        self.s_idx = a("s_idx", [128, 128], I32)
        self.s_onesb = a("s_onesb", [128, 128], BF16)
        self.s_onesf = a("s_onesf", [128, 128], F32)
        self.s_bf = a("s_bf", [32, 1], F32)
        self.s_small = a("s_small", [128, 64], F32)
        self.s_const = a("s_const", [128, 8], F32)
        self.cvals = {}

        self.ps = [nc.alloc_psum_tensor(f"ps{i}", [128, 512], F32) for i in range(6)]
        self.pst = [nc.alloc_psum_tensor(f"pst{i}", [128, 1024], BF16) for i in range(2)]

    def cst(self, val):
        if val not in self.cvals:
            self.cvals[val] = len(self.cvals)
            assert len(self.cvals) <= 8
        return self.s_const.ap()[:, self.cvals[val]:self.cvals[val] + 1]

    def xn(self, k, c0, n):
        return self.s_xn.ap()[:, k * 1024 + c0: k * 1024 + c0 + n]

    def dbg_add(self, name, shape, dtype):
        t = self.nc.dram_tensor("dbg_" + name, shape, dtype, kind="ExternalOutput")
        self.dbg_outs[name] = t
        return t

    def dbg_dump_sbuf(self, name, ap, shape, dtype, waits):
        t = self.dbg_add(name, shape, dtype)
        return self.P.dma("gpsimd", t.ap(), ap, waits=waits, inc="g")

    def prolog(self):
        P = self.P
        evs = []
        evs.append(P.dma("sync", self.s_identb[:], self.identb_in.ap(), inc="pro"))
        evs.append(P.dma("sync", self.s_identf[:], self.identf_in.ap(), inc="pro"))
        evs.append(P.dma("sync", self.s_cs[:], self.cs_in.ap(), inc="pro"))
        evs.append(P.dma("sync", self.s_mcol[:], self.mcol_in.ap(), inc="pro"))
        evs.append(P.dma("sync", self.s_bf[:], self.w["L1_attn_bf"].ap(), inc="pro"))
        evs.append(P.dma("sync", self.s_tab[:],
                         self.tab_in.ap().rearrange("a b -> b a"), inc="pro"))
        evs.append(P.dma("sync", self.s_idx[:], self.idx_in.ap(), inc="pro"))
        evs.append(P.dma("sync", self.s_masks.ap().rearrange("p (a c) -> p a c", a=4),
                         self.masks_in.ap().rearrange("(a p) c -> p a c", p=128), inc="pro"))
        evs.append(P.dma("sync", self.s_sel0.ap().rearrange("p (a c) -> p a c", a=17),
                         self.sel0_in.ap().rearrange("(a p) c -> p a c", p=128), inc="pro"))
        evs.append(P.dma("sync", self.s_sel1.ap().rearrange("p (a c) -> p a c", a=16),
                         self.sel1_in.ap().rearrange("(a p) c -> p a c", p=128), inc="pro"))
        evs.append(P.dma("sync", self.s_gain.ap()[:, 0:12], self.gq_in.ap(), inc="pro"))
        evs.append(P.dma("sync", self.s_gain.ap()[:, 12:16], self.gkv_in.ap(), inc="pro"))
        def fill_consts(e):
            ins = None
            for val, idx in self.cvals.items():
                ins = e.memset(self.s_const.ap()[:, idx:idx + 1], float(val))
            if ins is None:
                ins = e.memset(self.s_const.ap()[:, 0:1], 0.0)
            return ins
        ec = P.do("vector", fill_consts, inc="v")
        e1 = P.do("vector", lambda e: e.memset(self.s_onesb[:], 1.0), inc="v")
        e2 = P.do("vector", lambda e: e.memset(self.s_onesf[:], 1.0), inc="v")
        ez = P.do("vector", lambda e: e.memset(self.s_aux[:], 0.0), inc="v")
        for r0 in range(8 * 2240, 9 * 2240, 112):
            evs.append(P.dma("sync",
                             self.d_kout.ap()[r0:r0 + 112, :],
                             self.s_aux.ap()[0:112, 0:1024], waits=[ez], inc="pro"))
        for i in range(8):
            evs.append(P.dma("sync",
                             self.d_vout.ap()[8 * 1024 + i * 128: 8 * 1024 + (i + 1) * 128, :],
                             self.s_aux.ap()[:, 0:2048], waits=[ez], inc="pro"))
        evs.append(P.dma("sync", self.d_rout.ap()[8 * 64: 9 * 64, :],
                         self.s_aux.ap()[0:64, 0:1024], waits=[ez], inc="pro"))
        evs.append(P.dma("sync", self.d_kin.ap()[2048:2176, :],
                         self.s_aux.ap()[:, 0:1024], waits=[ez], inc="pro"))
        evs.append(P.dma("sync", self.d_kin.ap()[2176:2240, :],
                         self.s_aux.ap()[0:64, 0:1024], waits=[ez], inc="pro"))
        ezs = P.do("vector", lambda e: e.memset(self.s_stat.ap()[:, 0:512], 0.0), inc="v")
        self.ev_stg_zero = ezs
        self.prolog_evs = evs + [e1, e2, ec, ezs]
        self.ev_prolog_last = evs[-1]

    def stage_ln(self, lyr, ntiles=8):
        P = self.P
        src = self.x_in if lyr == 0 else self.d_x1
        f = self.s_f32.ap()
        xrow = [f[:, 0:2048], f[:, 2048:4096]]
        scr = f[:, 4096:6144]
        sm = self.s_small.ap()
        sums, ssq, mean, tmp, m2, var, rstd = (sm[:, i:i + 1] for i in range(7))
        xnrow = self.s_aux.ap()[:, 0:2048]
        ld = [None, None]
        last_use = [None, None]
        tr_ev = [None, None]
        prev_xn_done = None
        prev_ssq_read = None
        first_w = list(self.prolog_evs) + list(getattr(self, "ev_fold_done", []))
        for tt in range(ntiles):
            b = tt % 2
            ld[b] = P.dma("sync", xrow[b], src.ap()[tt * 128:(tt + 1) * 128, :],
                          waits=[last_use[b]] + (first_w if tt < 2 else []),
                          inc=f"lnx{b}")
            ev_s = P.do("vector", lambda e, b=b: e.tensor_reduce(sums, xrow[b], AX.X, ALU.add),
                        waits=[ld[b]], inc="v")
            ev_q = P.do("scalar",
                        lambda e, b=b: e.activation(scr, xrow[b], AF.Square, bias=0.0,
                                                    scale=1.0, accum_out=ssq),
                        waits=[ld[b], prev_ssq_read], inc="a")
            ev_m = P.do("vector", lambda e: e.tensor_scalar_mul(mean, sums, 1.0 / D),
                        waits=[ev_s], inc="v")
            prev_ssq_read = P.do("vector", lambda e: e.tensor_scalar_mul(tmp, ssq, 1.0 / D),
                                 waits=[ev_q], inc="v")
            ev_m2 = P.do("vector", lambda e: e.tensor_mul(m2, mean, mean),
                         waits=[ev_m], inc="v")
            ev_var = P.do("vector", lambda e: e.tensor_sub(var, tmp, m2),
                          waits=[ev_m2, prev_ssq_read], inc="v")
            ev_std = P.do("scalar", lambda e: e.activation(var, var, AF.Sqrt,
                                                           bias=self.cst(EPS_LN), scale=1.0),
                          waits=[ev_var], inc="a")
            ev_rs = P.do("vector", lambda e: e.reciprocal(rstd, var), waits=[ev_std], inc="v")
            ev_xn = P.do("vector",
                         lambda e, b=b: e.tensor_scalar(xnrow, xrow[b], mean, rstd,
                                                        ALU.subtract, ALU.mult),
                         waits=[ev_rs, ev_m, prev_xn_done], inc="v")
            last_use[b] = ev_xn
            if tt == 0 and "lnrow" in self.dbg:
                self.dbg_dump_sbuf("lnrow", xnrow, [128, 2048], BF16, waits=[ev_xn])
                self.dbg_dump_sbuf("lnstat", self.s_small.ap()[:, 0:8], [128, 8], F32,
                                   waits=[ev_xn])
                self.dbg_dump_sbuf("lnx", xrow[0], [128, 2048], F32, waits=[ev_xn])
            for k in range(16):
                tb = k % 2
                ev_t = P.do("tensor",
                            lambda e, k=k, tb=tb: e.transpose(self.pst[tb][:, 0:128],
                                                              xnrow[:, k * 128:(k + 1) * 128],
                                                              self.s_identb[:]),
                            waits=[ev_xn, tr_ev[tb]], inc="t")
                tr_ev[tb] = P.do("vector",
                                 lambda e, k=k, tt=tt, tb=tb: e.tensor_copy(
                                     self.xn(k, tt * 128, 128), self.pst[tb][:, 0:128]),
                                 waits=[ev_t], inc="v")
                if k == 15:
                    prev_xn_done = ev_t
        self.ev_ln = [tr_ev[0], tr_ev[1]]


    def proj_fm(self, w_dram, n_in, n_out, evac_fn, waits, wst_off=0,
                mw=128, eng_dma="gpsimd", rhs_fn=None, col0=0, w_ncols=None,
                tgs=(0, 1)):
        P = self.P
        if rhs_fn is None:
            rhs_fn = self.xn
        KCH = n_in // 128
        MCH = (n_out + mw - 1) // mw
        slab_elems = KCH * mw
        slab_free = [None, None]
        bank_free = [[], [], [], []]
        slab_ev = [None, None]
        out_evs = []
        wnc = w_ncols if w_ncols is not None else n_out
        for m in range(MCH):
            sb = m % 2
            cw = min(mw, n_out - m * mw)
            slab = self.s_wst.ap()[:, wst_off + sb * slab_elems:
                                   wst_off + sb * slab_elems + KCH * cw]
            slab3 = slab.rearrange("p (k c) -> p k c", k=KCH)
            src = bass.AP(w_dram, col0 + m * mw,
                          [[wnc, 128], [128 * wnc, KCH], [1, cw]])
            slab_ev[sb] = P.dma(eng_dma, slab3, src,
                                waits=[slab_free[sb]] + list(waits), inc=f"ws{sb}")
            for ti, tg in enumerate(tgs):
                bi = (len(tgs) * m + ti) % 4
                ps = self.ps[bi]
                mm_ev = None
                for k in range(KCH):
                    last = (k == KCH - 1)
                    def mmfn(e, k=k, cw=cw, bi=bi, slab3=slab3, tg=tg, last=last,
                             rhs_fn=rhs_fn):
                        return e.matmul(self.ps[bi][0:cw, :],
                                        slab3[:, k, :],
                                        rhs_fn(k, tg * 512, 512),
                                        start=(k == 0), stop=last)
                    wts = ([slab_ev[sb]] + bank_free[bi] + list(waits)) if k == 0 else []
                    mm_ev = P.do("tensor", mmfn, waits=wts,
                                 inc="t" if last else None)
                evs = evac_fn(m, tg, bi, self.ps[bi], mm_ev)
                bank_free[bi] = list(evs) if evs else [mm_ev]
                out_evs.extend(evs)
                if ti == len(tgs) - 1:
                    slab_free[sb] = mm_ev
        return out_evs

    def rms_stats(self, sq_chunks, nchunk, md, eps, tg, waits, bank, post_scale=None):
        P = self.P
        ps = self.ps[bank]
        ev = None
        for k in range(nchunk):
            def mmfn(e, k=k):
                return e.matmul(ps[0:1, :], self.s_onesb[:, 0:1], sq_chunks(k),
                                start=(k == 0), stop=(k == nchunk - 1))
            ev = P.do("tensor", mmfn, waits=list(waits) if k == 0 else [],
                      inc="t" if k == nchunk - 1 else None)
        rows = self.s_stat.ap()
        rs = rows[0:1, 1024 + tg * 512: 1024 + tg * 512 + 512]
        ev1 = P.do("vector", lambda e: e.tensor_scalar_mul(rs, ps[0:1, :], 1.0 / md),
                   waits=[ev], inc="v")
        ev2 = P.do("scalar", lambda e: e.activation(rs, rs, AF.Sqrt,
                                                    bias=self.cst(eps)[0:1, :], scale=1.0),
                   waits=[ev1], inc="a")
        ev3 = P.do("vector", lambda e: e.reciprocal(rs, rs), waits=[ev2], inc="v")
        if post_scale is not None:
            ev3 = P.do("vector", lambda e: e.tensor_scalar_mul(rs, rs, post_scale),
                       waits=[ev3], inc="v")
        return ev3, rs, ev1

    def bcast_row(self, row_ap, waits, bank):
        P = self.P
        stg = self.s_stat.ap()[:, 0:512]
        ev_c = P.do("vector", lambda e: e.tensor_copy(stg[0:1, :], row_ap),
                    waits=list(waits) + [getattr(self, "_bcast_last", None),
                                         self.ev_stg_zero], inc="v")
        ps = self.ps[bank]
        ev = P.do("tensor", lambda e: e.matmul(ps[:], self.s_onesf[:], stg,
                                               start=True, stop=True),
                  waits=[ev_c], inc="t")
        self._bcast_last = ev
        return ev, ps

    def stage_kv0(self):
        P = self.P
        W = self.s_work.ap()
        ckv_raw = lambda k, tg: W[:, k * 1024 + tg * 512: k * 1024 + tg * 512 + 512]
        ckv_sq = lambda k, tg: W[:, 4096 + k * 1024 + tg * 512: 4096 + k * 1024 + tg * 512 + 512]
        krope = W[:, 8192:9216]
        sqrope = W[:, 9216:10240]
        ckv_bf = lambda k, tg: W[:, 10240 + k * 1024 + tg * 512: 10240 + k * 1024 + tg * 512 + 512]
        base_w = list(self.ev_ln) + [self.ev_prolog_last]
        evs_rope = []
        rope_done = []

        def evac_kva(m, tg, bi, ps, ev_mm):
            out = []
            if getattr(self, "kv0_level", 3) == 0:
                npart = 128 if m < 4 else 64
                e1 = P.do("vector", lambda e, m=m, tg=tg, npart=npart: e.tensor_copy(
                    ckv_raw(min(m, 3), tg)[0:npart, :], ps[0:npart, :]),
                    waits=[ev_mm], inc="v")
                rope_done.append(e1)
                return [e1]
            if m < 4:
                e1 = P.do("vector", lambda e, m=m, tg=tg: e.tensor_copy(ckv_raw(m, tg), ps[:]),
                          waits=[ev_mm], inc="v")
                e2 = P.do("scalar", lambda e, m=m, tg=tg: e.activation(
                    ckv_sq(m, tg), ckv_raw(m, tg), AF.Square, bias=0.0, scale=1.0),
                          waits=[e1], inc="a")
                out = [e1, e2]
            elif getattr(self, "kv0_level", 3) < 2:
                e1 = P.do("vector", lambda e: e.tensor_copy(
                    krope[0:64, tg * 512: tg * 512 + 512], ps[0:64, :]),
                    waits=[ev_mm], inc="v")
                rope_done.append(e1)
                out = [e1]
            else:
                f = self.s_f32.ap()
                t1 = f[0:32, 4096 + tg * 512: 4608 + tg * 512]
                t2 = f[0:32, 5120 + tg * 512: 5632 + tg * 512]
                t3 = f[32:64, 4096 + tg * 512: 4608 + tg * 512]
                t4 = f[32:64, 5120 + tg * 512: 5632 + tg * 512]
                cosr = self.s_cs.ap()[0:32, tg * 512: tg * 512 + 512]
                sinr = self.s_cs.ap()[32:64, tg * 512: tg * 512 + 512]
                a_ = ps[0:32, :]
                b_ = ps[32:64, :]
                e1 = P.do("vector", lambda e: e.tensor_mul(t1, a_, cosr), waits=[ev_mm], inc="v")
                e2 = P.do("vector", lambda e: e.tensor_mul(t2, b_, sinr), waits=[ev_mm], inc="v")
                e3 = P.do("vector", lambda e: e.tensor_mul(t3, a_, sinr), waits=[ev_mm], inc="v")
                e4 = P.do("vector", lambda e: e.tensor_mul(t4, b_, cosr), waits=[ev_mm], inc="v")
                e5 = P.do("vector", lambda e: e.tensor_sub(
                    krope[0:32, tg * 512: tg * 512 + 512], t1, t2), waits=[e1, e2], inc="v")
                e6 = P.do("vector", lambda e: e.tensor_add(
                    krope[32:64, tg * 512: tg * 512 + 512], t3, t4), waits=[e3, e4], inc="v")
                e7 = P.do("scalar", lambda e: e.activation(
                    sqrope[0:64, tg * 512: tg * 512 + 512],
                    krope[0:64, tg * 512: tg * 512 + 512],
                    AF.Square, bias=0.0, scale=1.0), waits=[e5, e6], inc="a")
                ed = P.dma("sync", self.d_rin.ap()[:, tg * 512: tg * 512 + 512],
                           krope[0:64, tg * 512: tg * 512 + 512], waits=[e5, e6], inc="kvst")
                evs_rope.append(ed)
                rope_done.append(e7)
                out = [e5, e6, e7]
            return out

        ev_z = P.do("vector", lambda e: e.memset(sqrope[:], 0.0), waits=base_w, inc="v")
        evs_a = self.proj_fm(self.w["L0_attn_wkv_a"], D, 576, evac_kva,
                             base_w + [ev_z], wst_off=0)
        if getattr(self, "kv0_level", 3) < 3:
            self._kv0_own_done = evs_a + rope_done
            self.ev_gather0 = None
            return
        stt_evs = []
        prev_stt = []
        for tg in range(2):
            ev_rs, rs, _ = self.rms_stats(lambda k, tg=tg: ckv_sq(k, tg), 4, KVLR,
                                          EPS_RMS, tg, evs_a, 4 + tg)
            ev_b, psb = self.bcast_row(rs, [ev_rs], 4 + tg)
            bc = self.s_stat.ap()[:, 512:1024]
            ev_bc = P.do("vector", lambda e, psb=psb: e.tensor_copy(bc, psb[:]),
                         waits=[ev_b] + prev_stt, inc="v")
            prev_stt = []
            for k in range(4):
                gcol = self.s_gain.ap()[:, 12 + k:13 + k]
                ev = P.do("vector",
                          lambda e, k=k, tg=tg, gcol=gcol: e.scalar_tensor_tensor(
                              ckv_bf(k, tg), ckv_raw(k, tg), gcol, bc,
                              ALU.mult, ALU.mult),
                          waits=[ev_bc], inc="v")
                stt_evs.append(ev)
                prev_stt.append(ev)
        ksb = [self.s_aux.ap()[:, i * 512:(i + 1) * 512] for i in range(2)]
        ksq = [self.s_aux.ap()[:, 1024 + i * 512: 1536 + i * 512] for i in range(2)]
        vsb = [self.s_aux.ap()[:, 2048 + i * 512: 2560 + i * 512] for i in range(2)]
        vt = [self.s_aux.ap()[:, 3072 + i * 512: 3584 + i * 512] for i in range(2)]
        kin_evs = []
        vin_evs = []
        self._kv0_state = dict(ksb_free={}, vsb_free={}, norm_mm=[None, None],
                               pst_free=[None, None], vt_free=[None, None])
        st = self._kv0_state

        def xn_ckv(k, c0, n):
            return W[:, 10240 + k * 1024 + c0: 10240 + k * 1024 + c0 + n]

        def evac_kvb(m, tg, bi, ps, ev_mm):
            h = m // 2
            if m % 2 == 0:
                buf = (m // 2 + tg) % 2
                e1 = P.do("vector", lambda e, buf=buf: e.tensor_copy(ksb[buf], ps[:]),
                          waits=[ev_mm, st["ksb_free"].get(buf)], inc="v")
                e2 = P.do("scalar", lambda e, buf=buf: e.activation(ksq[buf], ksb[buf],
                                                                   AF.Square,
                                                                   bias=0.0, scale=1.0),
                          waits=[e1, st["ksb_free"].get((buf, "q"))], inc="a")
                ed = P.dma("sync", self.d_kin.ap()[h * 128:(h + 1) * 128,
                                                   tg * 512: tg * 512 + 512],
                           ksb[buf], waits=[e1], inc=f"ksb{buf}")
                kin_evs.append(ed)
                def nmm(e, h=h, tg=tg, buf=buf):
                    return e.matmul(self.ps[4 + tg][0:16, :],
                                    self.s_sel0.ap()[:, h * 16:(h + 1) * 16],
                                    ksq[buf], start=(h == 0), stop=False)
                en = P.do("tensor", nmm, waits=[e2] + ([st["norm_mm"][tg]] if h == 0 else []),
                          inc="t")
                st["norm_mm"][tg] = en
                st["ksb_free"][buf] = ed
                st["ksb_free"][(buf, "q")] = en
                return [ed, en]
            else:
                buf = (m // 2 + tg) % 2
                e1 = P.do("vector", lambda e: e.tensor_copy(vsb[buf], ps[:]),
                          waits=[ev_mm, st["vsb_free"].get(buf)], inc="v")
                evs = []
                for i in range(4):
                    pb = i % 2
                    et = P.do("tensor",
                              lambda e, i=i, pb=pb, buf=buf: e.transpose(
                                  self.pst[pb][:, 0:128],
                                  vsb[buf][:, i * 128:(i + 1) * 128],
                                  self.s_identb[:]),
                              waits=[e1, st["pst_free"][pb]], inc="t")
                    ev = P.do("vector",
                              lambda e, i=i, pb=pb, buf=buf: e.tensor_copy(
                                  vt[buf][:, i * 128:(i + 1) * 128],
                                  self.pst[pb][:, 0:128]),
                              waits=[et, st["vt_free"][buf]], inc="v")
                    st["pst_free"][pb] = ev
                    evs.append(ev)
                ed = P.dma("sync",
                           self.d_vin.ap()[tg * 512: tg * 512 + 512,
                                           h * 128:(h + 1) * 128]
                           .rearrange("(a p) c -> p a c", p=128),
                           vt[buf].rearrange("p (a c) -> p a c", a=4),
                           waits=evs, inc=f"vt{buf}")
                vin_evs.append(ed)
                st["vsb_free"][buf] = evs[-1]
                st["vt_free"][buf] = ed
                return [ed]

        if getattr(self, "kv0_half", False):
            self._kv0_own_done = evs_rope + stt_evs
            self.ev_gather0 = None
            return
        evs_b = self.proj_fm(self.w["L0_attn_wkv_b"], KVLR, 4096, evac_kvb,
                             stt_evs, wst_off=4096, rhs_fn=xn_ckv)
        nm_evs = []
        for tg in range(2):
            def nmm(e, tg=tg):
                return e.matmul(self.ps[4 + tg][0:16, :],
                                self.s_sel0.ap()[:, 16 * 16: 17 * 16],
                                sqrope[:, tg * 512: tg * 512 + 512],
                                start=False, stop=True)
            en = P.do("tensor", nmm, waits=[st["norm_mm"][tg]] + rope_done, inc="t")
            nm_evs.append(en)
        sm = self.s_small.ap()
        km0, km1, km = sm[0:16, 16:17], sm[0:16, 17:18], sm[0:16, 18:19]
        e1 = P.do("vector", lambda e: e.tensor_reduce(km0, self.ps[4][0:16, :], AX.X, ALU.max),
                  waits=[nm_evs[0]], inc="v")
        e2 = P.do("vector", lambda e: e.tensor_reduce(km1, self.ps[5][0:16, :], AX.X, ALU.max),
                  waits=[nm_evs[1]], inc="v")
        e3 = P.do("vector", lambda e: e.tensor_max(km, km0, km1), waits=[e1, e2], inc="v")
        ez = P.do("vector", lambda e: e.memset(sm[0:32, 19:20], 0.0), inc="v")
        e4 = P.do("vector", lambda e: e.tensor_copy(sm[0:16, 19:20], km), waits=[e3, ez], inc="v")
        en_dma = P.dma("sync", self.d_nin.ap(), sm[0:32, 19:20], waits=[e4], inc="kvst")
        gw = kin_evs + vin_evs + evs_rope + [en_dma]
        self._kv0_own_done = gw
        self.ev_kv0_all = list(evs_b) + nm_evs + [en_dma, e4]
        if getattr(self, "skip_gather", False):
            self.ev_gather0 = None
        else:
            self.ev_gather0 = self.gathers(gw, "0")

    def stage_q0(self):
        P = self.P
        W = self.s_work.ap()
        A = self.s_aux.ap()
        xq_raw = lambda k, tg: W[:, k * 1024 + tg * 512: k * 1024 + tg * 512 + 512]
        xq_sq = lambda k, tg: W[:, 12288 + k * 1024 + tg * 512: 12288 + k * 1024 + tg * 512 + 512]
        q_lr = lambda k, c0, n: W[:, 12288 + k * 1024 + c0: 12288 + k * 1024 + c0 + n]
        qstage = [A[:, i * 512:(i + 1) * 512] for i in range(2)]
        qsq = [A[:, 1024 + i * 512: 1536 + i * 512] for i in range(2)]
        ropest = [A[:, 2048 + i * 512: 2560 + i * 512] for i in range(2)]
        ropesq = [A[:, 3072 + i * 512: 3584 + i * 512] for i in range(2)]
        crow_b = A[0:16, 4096:5120]
        base_w = list(self.ev_kv0_all)

        def evac_qa(m, tg, bi, ps, ev_mm):
            e1 = P.do("vector", lambda e, m=m, tg=tg: e.tensor_copy(xq_raw(m, tg), ps[:]),
                      waits=[ev_mm] + (base_w if m < 2 else []), inc="v")
            e2 = P.do("scalar", lambda e, m=m, tg=tg: e.activation(
                xq_sq(m, tg), xq_raw(m, tg), AF.Square, bias=0.0, scale=1.0),
                waits=[e1], inc="a")
            return [e1, e2]

        evs_a = self.proj_fm(self.w["L0_attn_wq_a"], D, QLR, evac_qa,
                             base_w, wst_off=0)
        stt_evs = []
        prev_stt = []
        for tg in range(2):
            ev_rs, rs, _ = self.rms_stats(lambda k, tg=tg: xq_sq(k, tg), 12, QLR,
                                          EPS_RMS, tg, evs_a, 4 + tg, post_scale=SC0)
            ev_b, psb = self.bcast_row(rs, [ev_rs], 4 + tg)
            bc = self.s_stat.ap()[:, 512:1024]
            ev_bc = P.do("vector", lambda e, psb=psb: e.tensor_copy(bc, psb[:]),
                         waits=[ev_b] + prev_stt, inc="v")
            prev_stt = []
            for k in range(12):
                gcol = self.s_gain.ap()[:, k:k + 1]
                ev = P.do("vector",
                          lambda e, k=k, tg=tg, gcol=gcol: e.scalar_tensor_tensor(
                              q_lr(k, tg * 512, 512), xq_raw(k, tg), gcol, bc,
                              ALU.mult, ALU.mult),
                          waits=[ev_bc], inc="v")
                stt_evs.append(ev)
                prev_stt.append(ev)

        st = dict(qs_free={}, norm_mm=[None, None], dma_evs=[])

        def evac_qb_nope(m, tg, bi, ps, ev_mm):
            buf = (m + tg) % 2
            e1 = P.do("vector", lambda e, buf=buf: e.tensor_copy(qstage[buf], ps[:]),
                      waits=[ev_mm, st["qs_free"].get(("n", buf))], inc="v")
            ed = P.dma("sync", self.d_q.ap()[m * 128:(m + 1) * 128,
                                             tg * 512: tg * 512 + 512],
                       qstage[buf], waits=[e1], inc=f"qst{buf}")
            e2 = P.do("scalar", lambda e, buf=buf: e.activation(
                qsq[buf], qstage[buf], AF.Square, bias=0.0, scale=1.0),
                waits=[e1, st["qs_free"].get(("nq", buf))], inc="a")

            def nmm(e, m=m, tg=tg, buf=buf):
                return e.matmul(self.ps[4 + tg][0:16, :],
                                self.s_sel0.ap()[:, m * 16:(m + 1) * 16],
                                qsq[buf], start=(m == 0), stop=False)
            en = P.do("tensor", nmm, waits=[e2] + ([st["norm_mm"][tg]] if m == 0 else []),
                      inc="t")
            st["norm_mm"][tg] = en
            st["qs_free"][("n", buf)] = ed
            st["qs_free"][("nq", buf)] = en
            st["dma_evs"].append(ed)
            return [ed, en]

        evs_n = self.proj_fm(self.w["L0_attn_wq_b"], QLR, 2048, evac_qb_nope,
                             stt_evs, wst_off=4096, rhs_fn=q_lr, w_ncols=3072)

        ev_zrq = P.do("vector", lambda e: e.memset(ropesq[0][:], 0.0), inc="v")
        ev_zrq2 = P.do("vector", lambda e: e.memset(ropesq[1][:], 0.0), inc="v")

        def evac_qb_rope(j, tg, bi, ps, ev_mm):
            buf = (j + tg) % 2
            f = self.s_f32.ap()
            t1 = f[0:32, 4096 + tg * 512: 4608 + tg * 512]
            t2 = f[0:32, 5120 + tg * 512: 5632 + tg * 512]
            t3 = f[32:64, 4096 + tg * 512: 4608 + tg * 512]
            t4 = f[32:64, 5120 + tg * 512: 5632 + tg * 512]
            cosr = self.s_cs.ap()[0:32, tg * 512: tg * 512 + 512]
            sinr = self.s_cs.ap()[32:64, tg * 512: tg * 512 + 512]
            a_, b_ = ps[0:32, :], ps[32:64, :]
            wpre = [ev_mm, st["qs_free"].get(("rt", tg))]
            e1 = P.do("vector", lambda e: e.tensor_mul(t1, a_, cosr), waits=wpre, inc="v")
            e2 = P.do("vector", lambda e: e.tensor_mul(t2, b_, sinr), waits=wpre, inc="v")
            e3 = P.do("vector", lambda e: e.tensor_mul(t3, a_, sinr), waits=wpre, inc="v")
            e4 = P.do("vector", lambda e: e.tensor_mul(t4, b_, cosr), waits=wpre, inc="v")
            wr = st["qs_free"].get(("r", buf))
            e5 = P.do("vector", lambda e, buf=buf: e.tensor_sub(
                ropest[buf][0:32, :], t1, t2), waits=[e1, e2, wr], inc="v")
            e6 = P.do("vector", lambda e, buf=buf: e.tensor_add(
                ropest[buf][32:64, :], t3, t4), waits=[e3, e4, wr], inc="v")
            st["qs_free"][("rt", tg)] = e6
            ed = P.dma("sync", self.d_q.ap()[2048 + j * 64: 2048 + (j + 1) * 64,
                                             tg * 512: tg * 512 + 512],
                       ropest[buf][0:64, :], waits=[e5, e6], inc=f"qsr{buf}")
            e7 = P.do("scalar", lambda e, buf=buf: e.activation(
                ropesq[buf][0:64, :], ropest[buf][0:64, :], AF.Square,
                bias=0.0, scale=1.0),
                waits=[e5, e6, st["qs_free"].get(("rq", buf))], inc="a")

            def nmm(e, j=j, tg=tg, buf=buf):
                return e.matmul(self.ps[4 + tg][0:16, :],
                                self.s_sel0.ap()[:, j * 16:(j + 1) * 16],
                                ropesq[buf][:], start=False, stop=(j == 15))
            en = P.do("tensor", nmm, waits=[e7, st["norm_mm"][tg]], inc="t")
            st["norm_mm"][tg] = en
            st["qs_free"][("r", buf)] = ed
            st["qs_free"][("rq", buf)] = en
            st["dma_evs"].append(ed)
            return [ed, en]

        evs_r = self.proj_fm(self.w["L0_attn_wq_b"], QLR, 1024, evac_qb_rope,
                             stt_evs + [ev_zrq, ev_zrq2], wst_off=6144,
                             rhs_fn=q_lr, mw=64, col0=2048, w_ncols=3072)

        qn = self.s_stat.ap()[0:16, 1024:2048]
        ev_qn = []
        for tg in range(2):
            ev = P.do("scalar",
                      lambda e, tg=tg: e.activation(qn[:, tg * 512: tg * 512 + 512],
                                                    self.ps[4 + tg][0:16, :],
                                                    AF.Sqrt, bias=0.0, scale=1.0),
                      waits=[st["norm_mm"][tg]], inc="a")
            ev_qn.append(ev)
        sm = self.s_small.ap()
        kS = sm[0:16, 20:21]
        ev_ld = P.dma("sync", sm[0:32, 21:22], self.d_nout.ap(),
                      waits=[self.ev_gather0["n"]], inc="kvst")
        ev_ks = P.do("scalar", lambda e: e.activation(kS, sm[0:16, 21:22], AF.Sqrt,
                                                      bias=0.0, scale=1.0),
                     waits=[ev_ld], inc="a")
        ev_cr = P.do("vector",
                     lambda e: e.tensor_scalar(crow_b, qn, kS, -1.0,
                                               ALU.mult, ALU.mult),
                     waits=[ev_ks] + ev_qn, inc="v")
        ev_crd = P.dma("sync", self.d_q.ap()[3072:3088, :], crow_b,
                       waits=[ev_cr], inc="qst0")
        self.ev_q0 = [ev_crd] + st["dma_evs"][-2:]
        self.ev_q0_all = st["dma_evs"] + [ev_crd]


    def stage_ffn(self, lyr, waits):
        P = self.P
        W = self.s_work.ap()
        A = self.s_aux.ap()
        wg = self.w[f"L{lyr}_ffn_w_gate"]
        wu = self.w[f"L{lyr}_ffn_w_up"]
        wd = self.w[f"L{lyr}_ffn_w_down"]
        hsp3 = W.rearrange("p (k c) -> p k c", k=64)
        silu_st = A[:, 0:256]
        h_st = A[:, 256:512]
        absh = A[:, 512:768]
        mask_st = A[:, 768:1024]
        hi = A[:, 1024:1152]
        lo = A[:, 1152:1280]
        mnhi = A[:, 1280:1344]
        mxlo = A[:, 1344:1408]
        t2 = A[:, 1408:1472]
        gst = [A[:, 2048 + i * 512: 2560 + i * 512] for i in range(2)]
        dg_evs = []
        prev = dict(slab=[None, None], silu=None, h=None, sp=None, tpos=[None, None],
                    hspw={}, gfree={}, dphase=[])
        for tg in range(2):
            base = list(waits) if tg == 0 else list(prev["dphase"])
            prev["dphase"] = []
            gs_evs = []
            for j in range(32):
                sb = j % 2
                gslab = self.s_wst.ap()[:, sb * 8192: sb * 8192 + 4096]
                uslab = self.s_wst.ap()[:, sb * 8192 + 4096: sb * 8192 + 8192]
                g3 = gslab.rearrange("p (k c) -> p k c", k=16)
                u3 = uslab.rearrange("p (k c) -> p k c", k=16)
                src_g = bass.AP(wg, j * 256, [[8192, 128], [128 * 8192, 16], [1, 256]])
                src_u = bass.AP(wu, j * 256, [[8192, 128], [128 * 8192, 16], [1, 256]])
                ev_g = P.dma("gpsimd", g3, src_g,
                             waits=[prev["slab"][sb]] + base, inc=f"ws{sb}")
                ev_u = P.dma("gpsimd", u3, src_u, waits=[], inc=f"ws{sb}")
                mm_u_last = None
                for tt in range(4):
                    mm_g = mm_u = None
                    for k in range(16):
                        def mfg(e, k=k, tg=tg, tt=tt, g3=g3):
                            return e.matmul(self.ps[4][:, 0:256],
                                            self.xn(k, tg * 512 + tt * 128, 128),
                                            g3[:, k, :], start=(k == 0), stop=(k == 15))
                        def mfu(e, k=k, tg=tg, tt=tt, u3=u3):
                            return e.matmul(self.ps[5][:, 0:256],
                                            self.xn(k, tg * 512 + tt * 128, 128),
                                            u3[:, k, :], start=(k == 0), stop=(k == 15))
                        w0 = [ev_g, ev_u, prev["silu"], prev["h"]] + (base if k == 0 else [])
                        mm_g = P.do("tensor", mfg, waits=w0 if k == 0 else [],
                                    inc="t" if k == 15 else None)
                        mm_u = P.do("tensor", mfu, inc="t" if k == 15 else None)
                    mm_u_last = mm_u
                    e_si = P.do("scalar", lambda e: e.activation(
                        silu_st, self.ps[4][:, 0:256], AF.Silu, bias=0.0, scale=1.0),
                        waits=[mm_g, prev["h"]], inc="a")
                    prev["silu"] = e_si
                    e_h = P.do("vector", lambda e: e.tensor_mul(
                        h_st, self.ps[5][:, 0:256], silu_st),
                        waits=[mm_u, e_si, prev["sp"]], inc="v")
                    prev["h"] = e_h
                    h3a = h_st.rearrange("p (a b) -> p a b", b=2)
                    e_hi = P.do("vector", lambda e: e.tensor_reduce(
                        hi, h3a, AX.X, ALU.max, apply_absolute_value=True),
                        waits=[e_h], inc="v")
                    e_lo = P.do("vector", lambda e: e.tensor_reduce(
                        lo, h3a, AX.X, ALU.min, apply_absolute_value=True),
                        waits=[e_h], inc="v")
                    e_mn = P.do("vector", lambda e: e.tensor_reduce(
                        mnhi, hi.rearrange("p (a b) -> p a b", b=2), AX.X, ALU.min),
                        waits=[e_hi], inc="v")
                    e_mx = P.do("vector", lambda e: e.tensor_reduce(
                        mxlo, lo.rearrange("p (a b) -> p a b", b=2), AX.X, ALU.max),
                        waits=[e_lo], inc="v")
                    e_t2 = P.do("vector", lambda e: e.tensor_max(t2, mnhi, mxlo),
                                waits=[e_mn, e_mx], inc="v")
                    e_ab = P.do("scalar", lambda e: e.activation(
                        absh, h_st, AF.Abs, bias=0.0, scale=1.0), waits=[e_h], inc="a")
                    t2b = bass.AP(self.s_aux, 1408,
                                  [[6144, 128], [1, 64], [0, 4]])
                    e_mk = P.do("vector", lambda e: e.tensor_tensor(
                        mask_st.rearrange("p (a b) -> p a b", b=4),
                        absh.rearrange("p (a b) -> p a b", b=4),
                        t2b, ALU.is_ge), waits=[e_t2, e_ab], inc="v")
                    e_sp = P.do("vector", lambda e: e.tensor_mul(h_st, h_st, mask_st),
                                waits=[e_mk], inc="v")
                    prev["sp"] = e_sp
                    for half in range(2):
                        pb = half
                        ic = 2 * j + half
                        e_t = P.do("tensor", lambda e, half=half, pb=pb: e.transpose(
                            self.pst[pb][:, 0:128],
                            h_st[:, half * 128:(half + 1) * 128], self.s_identb[:]),
                            waits=[e_sp, prev["tpos"][pb]], inc="t")
                        e_w = P.do("vector", lambda e, ic=ic, tt=tt, pb=pb: e.tensor_copy(
                            hsp3[:, ic, tt * 128:(tt + 1) * 128], self.pst[pb][:, 0:128]),
                            waits=[e_t, prev["hspw"].get(ic)], inc="v")
                        prev["tpos"][pb] = e_w
                        prev["hspw"][ic] = e_w
                        gs_evs.append(e_w)
                prev["slab"][sb] = mm_u_last

            def rhs_hsp(k, c0, n, tg=tg):
                return hsp3[:, k, c0 - tg * 512: c0 - tg * 512 + n]

            def evac_down(m, tg2, bi, ps, ev_mm):
                buf = m % 2
                e1 = P.do("scalar", lambda e, buf=buf: e.activation(
                    gst[buf], ps[:], AF.Silu, bias=0.0, scale=1.0),
                    waits=[ev_mm, prev["gfree"].get(buf)], inc="a")
                ed = P.dma("sync", self.d_g.ap()[m * 128:(m + 1) * 128,
                                                 tg2 * 512: tg2 * 512 + 512],
                           gst[buf], waits=[e1], inc=f"gst{buf}")
                prev["gfree"][buf] = ed
                dg_evs.append(ed)
                prev["dphase"].append(ed)
                return [e1]

            evs_d = self.proj_fm(wd, 8192, D, evac_down, gs_evs,
                                 wst_off=0, rhs_fn=rhs_hsp, tgs=(tg,))
            prev["dphase"].extend(evs_d)
            fence = evs_d[-1]
            prev["hspw"] = {ic: fence for ic in range(64)}
        self.ev_ffn = dg_evs


    def _reg(self, e, name):
        key = (str(e.engine), name)
        if key not in self._regs:
            self._regs[key] = e.alloc_register(name)
        return self._regs[key]

    def stage_vreorder(self, lyr, waits):
        P = self.P
        A = self.s_aux.ap()
        stg = [A[:, 0:2048], A[:, 2048:4096]]
        free = [None, None]
        evs = []
        for j in range(72):
            b = j % 2

            def gfn(e, j=j, b=b):
                return e.indirect_dma_start(
                    stg[b], None, self.d_vout.ap(),
                    bass.IndirectOffsetOnAxis(self.s_idx.ap()[0:128, 36 + j:37 + j], 0))
            eg = P.do("gpsimd", gfn, waits=[free[b]] + (list(waits) if j < 2 else []),
                      inc=f"vro{b}{lyr}", amt=16)
            es = P.dma("gpsimd", self.d_vord.ap()[j * 128:(j + 1) * 128, :], stg[b],
                       waits=[eg], inc=f"vrs{b}{lyr}")
            free[b] = es
            evs.append(es)
        return evs

    def stage_qslots(self, lyr, waits):
        P = self.P
        self._regs = getattr(self, "_regs", {})
        rows = 3088 if lyr == 0 else 2144
        evs = []
        for s8 in range(8):
            def fn(e, s8=s8, rows=rows):
                r0 = self._reg(e, "rq0")
                e.load(r0, self.s_tab.ap()[0:1, s8:s8 + 1])
                return e.dma_start(
                    bass.AP(self.d_qslots, s8 * 512, [[4096, rows], [1, 512]]),
                    bass.AP(self.d_q, r0, [[1024, rows], [1, 512]]))
            evs.append(P.do("gpsimd", fn, waits=list(waits) if s8 == 0 else [],
                            inc=f"qsl{lyr}", amt=16))
        return evs

    def stage_attn(self, lyr, waits, kw=None, rw0=None, vw=None, qw=None):
        P = self.P
        self._regs = getattr(self, "_regs", {})
        W = self.s_work.ap()
        F = self.s_f32.ap()
        H = H0 if lyr == 0 else H1
        hd = 128 if lyr == 0 else 67
        dv = 128 if lyr == 0 else 64
        dvp = 128 if lyr == 0 else 65
        WVB = [4096, 5120]
        WDV = 8192
        kb = [W[:, 0:1024], W[:, 1024:2048]]
        rb = [W[:, 2048:3072], W[:, 3072:4096]]
        expb = [W[:, 6144:6656], W[:, 6656:7168]]
        dkb = W[:, 7168:7680]
        drb = W[:, 7680:8192]
        rhsN = W[:, 8704:9216]
        rhsR = W[:, 9216:9728]
        drhsN = W[:, 9728:10752]
        drhsR = W[:, 10752:11776]
        obuf = [W[:, 11776:12288], W[:, 12288:12800]]
        ost = [F[:, i * 512:(i + 1) * 512] for i in range(4)]
        zst = [F[0:2, 2048 + i * 512: 2048 + (i + 1) * 512] for i in range(2)]
        o_fin = F[:, 3072:4096]
        zf = [F[0:1, 4096:4608], F[0:1, 4608:5120]]
        zr = F[0:1, 5120:6144]
        zs = F[:, 6144:6656]
        bsb = self.s_stat.ap()[:, 512:1024]
        sem = lambda n: f"{n}{lyr}"
        kview = self.d_kout.ap().rearrange("r (h c) -> (r h) c", h=2)
        rview = self.d_rout.ap().rearrange("r (h c) -> (r h) c", h=2)
        IOA = bass.IndirectOffsetOnAxis
        IX = self.s_idx.ap()

        pre = [P.do("vector", lambda e: e.memset(zs, 0.0), waits=list(waits), inc="v")]

        def pmem(ap_, val):
            pre.append(P.do("vector", lambda e, ap_=ap_, val=val: e.memset(ap_, val),
                            waits=[pre[-1]], inc="v"))
        pmem(zst[0][0:2, :], 0.0)
        pmem(zst[1][0:2, :], 0.0)
        if lyr == 0:
            for t in (rb[0], rb[1], drb):
                pmem(t[64:128, :], 0.0)
                pmem(t[64:65, :], 1.0)
            for t in (rhsR, drhsR):
                pmem(t[64:128, :], 0.0)
        else:
            for t in (kb[0], kb[1], dkb):
                pmem(t[64:128, :], 0.0)
                pmem(t[96:99, :], 1.0)
            for b in range(2):
                pmem(bass.AP(self.s_work, WVB[b] + 64, [[32768, 128], [dvp, 8]]), 1.0)
            pmem(bass.AP(self.s_work, WDV + 64, [[32768, 128], [dvp, 4]]), 1.0)
            for t in (rhsN, drhsN[:, 0:512], drhsN[:, 512:1024]):
                pmem(t[64:128, :], 0.0)
                pmem(t[64:67, :], 1.0)

        st = dict(kb=[kw, kw], rb=[rw0, rw0], vb=[vw, vw],
                  psfree=[None, None], expfree=[None, None],
                  ps2free=None, ps3free=None,
                  chain=None, rhs=qw, drhs=None, dg=kw, dvu=vw,
                  ob=[None, None], ofin=None, zfin=None,
                  stg=[None] * 4, zstg=[None, None])
        o_evs = []

        for h in range(H):
            ch = P.dma("gpsimd", self.d_oacc.ap()[0:128, :], zs,
                       waits=[st["chain"], st["ofin"]] + (pre if h == 0 else []),
                       inc=sem("oac"))
            ch = P.dma("gpsimd", self.d_oacc.ap()[128:256, :], zs, waits=[], inc=sem("oac"))
            if lyr == 0:
                ch = P.dma("gpsimd", self.d_zacc.ap()[0:1, :], zs[0:1, 0:512],
                           waits=[st["zfin"]], inc=sem("oac"))
                ch = P.dma("gpsimd", self.d_zacc.ap()[1:2, :], zs[0:1, 0:512],
                           waits=[], inc=sem("oac"))
            st["chain"] = ch
            rw = [st["drhs"]] + (pre if h == 0 else [])
            if lyr == 0:
                P.dma("sync", drhsN, self.d_q.ap()[h * 128:(h + 1) * 128, :],
                      waits=rw, inc=sem("rh"))
                P.dma("sync", drhsR[0:64, :],
                      self.d_q.ap()[2048 + h * 64: 2048 + (h + 1) * 64, :],
                      waits=[], inc=sem("rh"))
                ev_drhs = P.dma("sync", drhsR[64:65, :],
                                self.d_q.ap()[3072 + h: 3073 + h, :],
                                waits=[], inc=sem("rh"))
            else:
                P.dma("sync", drhsN[0:64, :],
                      self.d_q.ap()[h * 64:(h + 1) * 64, :], waits=rw, inc=sem("rh"))
                ev_drhs = P.dma("sync", drhsN[96:99, :],
                                bass.AP(self.d_q, 2048 * 1024 + h * 1024,
                                        [[32 * 1024, 3], [1, 1024]]),
                                waits=[], inc=sem("rh"))

            for s8 in range(8):
                sb = s8 % 2
                rq_w = [st["rhs"]] + ([ev_drhs] if s8 == 0 else [])
                if lyr == 0:
                    e_q1 = P.dma("sync", rhsN,
                                 self.d_qslots.ap()[h * 128:(h + 1) * 128,
                                                    s8 * 512:(s8 + 1) * 512],
                                 waits=rq_w, inc=sem("rh"))
                    e_q2 = P.dma("sync", rhsR[0:64, :],
                                 self.d_qslots.ap()[2048 + h * 64: 2048 + (h + 1) * 64,
                                                    s8 * 512:(s8 + 1) * 512],
                                 waits=[], inc=sem("rh"))
                    e_rhs = P.dma("sync", rhsR[64:65, :],
                                  self.d_qslots.ap()[3072 + h: 3073 + h,
                                                     s8 * 512:(s8 + 1) * 512],
                                  waits=[], inc=sem("rh"))
                else:
                    e_q1 = P.dma("sync", rhsN[0:64, :],
                                 self.d_qslots.ap()[h * 64:(h + 1) * 64,
                                                    s8 * 512:(s8 + 1) * 512],
                                 waits=rq_w, inc=sem("rh"))
                    e_q2 = e_q1
                    e_rhs = P.dma("sync", rhsN[96:99, :],
                                  bass.AP(self.d_qslots,
                                          2048 * 4096 + h * 4096 + s8 * 512,
                                          [[32 * 4096, 3], [1, 512]]),
                                  waits=[], inc=sem("rh"))
                e_k = None
                for half in range(2):
                    if lyr == 0:
                        def kfn(e, s8=s8, half=half, sb=sb, h=h):
                            return e.indirect_dma_start(
                                kb[sb][0:128, half * 512:(half + 1) * 512], None,
                                kview, IOA(IX[0:128, s8 * 2 + half: s8 * 2 + half + 1], 0),
                                element_offset=h * 128 * 1024)
                        e_k = P.do("gpsimd", kfn,
                                   waits=[st["kb"][sb]] if half == 0 else [],
                                   inc=sem(f"kb{sb}"), amt=16)
                    else:
                        def kfn(e, s8=s8, half=half, sb=sb, h=h):
                            return e.indirect_dma_start(
                                kb[sb][0:64, half * 512:(half + 1) * 512], None,
                                kview, IOA(IX[0:64, s8 * 2 + half: s8 * 2 + half + 1], 0),
                                element_offset=h * 67 * 1024)
                        P.do("gpsimd", kfn,
                             waits=[st["kb"][sb]] if half == 0 else [],
                             inc=sem(f"kb{sb}"), amt=16)

                        def kfn2(e, s8=s8, half=half, sb=sb, h=h):
                            return e.indirect_dma_start(
                                kb[sb][64:67, half * 512:(half + 1) * 512], None,
                                kview, IOA(IX[0:3, s8 * 2 + half: s8 * 2 + half + 1], 0),
                                element_offset=(h * 67 + 64) * 1024)
                        e_k = P.do("gpsimd", kfn2, waits=[], inc=sem(f"kb{sb}"), amt=16)
                if lyr == 0:
                    e_rr = None
                    for half in range(2):
                        def rfn(e, s8=s8, half=half, sb=sb):
                            return e.indirect_dma_start(
                                rb[sb][0:64, half * 512:(half + 1) * 512], None,
                                rview,
                                IOA(IX[0:64, 18 + s8 * 2 + half: 19 + s8 * 2 + half], 0),
                                element_offset=0)
                        e_rr = P.do("gpsimd", rfn,
                                    waits=[st["rb"][sb]] if half == 0 else [],
                                    inc=sem(f"rb{sb}"), amt=16)
                else:
                    e_rr = e_k
                e_v = None
                for half in range(2):
                    e_v = P.dma("sync",
                                bass.AP(self.s_work, WVB[sb] + half * 4 * dvp,
                                        [[32768, 128], [dvp, 4], [1, dv]]),
                                self.d_vord.ap()[s8 * 1024 + half * 512:
                                                 s8 * 1024 + half * 512 + 512,
                                                 h * dv:(h + 1) * dv]
                                .rearrange("(a p) c -> p a c", p=128),
                                waits=[st["vb"][sb]] if half == 0 else [],
                                inc=sem(f"vb{sb}"))
                mmv_last = None
                mmz_e = None
                for c in range(8):
                    pp = c % 2

                    def mmfn1(e, c=c, sb=sb, one=(lyr == 1)):
                        return e.matmul(self.ps[c % 2][:, :],
                                        kb[sb][:, c * 128:(c + 1) * 128], rhsN,
                                        start=True, stop=one)
                    w0 = [st["psfree"][pp]]
                    if c == 0:
                        w0 += [e_k, e_q1, e_rhs, e_rr, e_q2]
                    mm_s = P.do("tensor", mmfn1, waits=w0,
                                inc="t" if lyr == 1 else None)
                    if lyr == 0:
                        def mmfn2(e, c=c, sb=sb):
                            return e.matmul(self.ps[c % 2][:, :],
                                            rb[sb][:, c * 128:(c + 1) * 128], rhsR,
                                            start=False, stop=True)
                        mm_s = P.do("tensor", mmfn2, inc="t")
                    bias_ap = self.s_mcol.ap()[:, s8 * 8 + c: s8 * 8 + c + 1]
                    e_e = P.do("scalar",
                               lambda e, pp=pp, bias_ap=bias_ap: e.activation(
                                   expb[pp], self.ps[pp][:, :], AF.Exp,
                                   bias=bias_ap, scale=1.0),
                               waits=[mm_s, st["expfree"][pp]], inc="a")
                    st["psfree"][pp] = e_e

                    def mmpv(e, c=c, sb=sb, dvp=dvp):
                        return e.matmul(self.ps[2][0:dvp, :],
                                        bass.AP(self.s_work, WVB[sb] + c * dvp,
                                                [[32768, 128], [1, dvp]]),
                                        expb[c % 2], start=(c == 0), stop=(c == 7))
                    wpv = [e_e] + ([e_v, st["ps2free"]] if c == 0 else [])
                    mmv = P.do("tensor", mmpv, waits=wpv, inc="t")
                    if lyr == 0:
                        def mmz(e, c=c):
                            return e.matmul(self.ps[3][0:1, :], self.s_onesb[:, 0:1],
                                            expb[c % 2], start=(c == 0), stop=(c == 7))
                        mmz_e = P.do("tensor", mmz,
                                     waits=[st["ps3free"]] if c == 0 else [], inc="t")
                        st["expfree"][pp] = mmz_e
                    else:
                        st["expfree"][pp] = mmv
                    mmv_last = mmv
                st["kb"][sb] = mmv_last
                st["rb"][sb] = mmv_last
                st["vb"][sb] = mmv_last
                st["rhs"] = mmv_last
                sg = s8 % 4
                e_ev = P.do("vector",
                            lambda e, sg=sg, dvp=dvp: e.tensor_copy(
                                ost[sg][0:dvp, :], self.ps[2][0:dvp, :]),
                            waits=[mmv_last, st["stg"][sg]], inc="v")
                st["ps2free"] = e_ev

                def scfn(e, s8=s8, sg=sg, dvp=dvp):
                    return e.indirect_dma_start(
                        self.d_oacc.ap(), IOA(IX[0:dvp, 108 + s8:109 + s8], 0),
                        ost[sg][0:dvp, :], None, compute_op=ALU.add)
                e_sc = P.do("gpsimd", scfn, waits=[e_ev, st["chain"]],
                            inc=sem("oac"), amt=16)
                st["chain"] = e_sc
                st["stg"][sg] = e_sc
                if lyr == 0:
                    zg = s8 % 2
                    e_ez = P.do("vector",
                                lambda e, zg=zg: e.tensor_copy(zst[zg][0:1, :],
                                                               self.ps[3][0:1, :]),
                                waits=[mmz_e, st["zstg"][zg]], inc="v")
                    st["ps3free"] = e_ez

                    def szfn(e, s8=s8, zg=zg):
                        return e.indirect_dma_start(
                            self.d_zacc.ap(), IOA(IX[0:2, 118 + s8:119 + s8], 0),
                            zst[zg], None, compute_op=ALU.add)
                    e_sz = P.do("gpsimd", szfn, waits=[e_ez, st["chain"]],
                                inc=sem("oac"), amt=16)
                    st["chain"] = e_sz
                    st["zstg"][zg] = e_sz

            for g in range(2):
                if lyr == 0:
                    def dkfn(e, g=g, h=h):
                        return e.indirect_dma_start(
                            dkb[0:128, :], None, kview, IOA(IX[0:128, 16 + g:17 + g], 0),
                            element_offset=h * 128 * 1024)
                    e_dk = P.do("gpsimd", dkfn, waits=[st["dg"]], inc=sem("dg"), amt=16)
                else:
                    def dkfn(e, g=g, h=h):
                        return e.indirect_dma_start(
                            dkb[0:64, :], None, kview, IOA(IX[0:64, 16 + g:17 + g], 0),
                            element_offset=h * 67 * 1024)
                    P.do("gpsimd", dkfn, waits=[st["dg"]], inc=sem("dg"), amt=16)

                    def dkfn2(e, g=g, h=h):
                        return e.indirect_dma_start(
                            dkb[64:67, :], None, kview, IOA(IX[0:3, 16 + g:17 + g], 0),
                            element_offset=(h * 67 + 64) * 1024)
                    e_dk = P.do("gpsimd", dkfn2, waits=[], inc=sem("dg"), amt=16)
                if lyr == 0:
                    def drfn(e, g=g):
                        return e.indirect_dma_start(
                            drb[0:64, :], None, rview, IOA(IX[0:64, 34 + g:35 + g], 0),
                            element_offset=0)
                    e_dk = P.do("gpsimd", drfn, waits=[], inc=sem("dg"), amt=16)
                e_dv = P.dma("sync",
                             bass.AP(self.s_work, WDV, [[32768, 128], [dvp, 4], [1, dv]]),
                             self.d_vord.ap()[8192 + g * 512: 8192 + (g + 1) * 512,
                                              h * dv:(h + 1) * dv]
                             .rearrange("(a p) c -> p a c", p=128),
                             waits=[st["dvu"]], inc=sem("dgv"))
                mmv_last = None
                mmz_e = None
                for j in range(4):
                    pp = j % 2

                    def dmm1(e, j=j, g=g, one=(lyr == 1)):
                        return e.matmul(self.ps[j % 2][:, :],
                                        dkb[:, j * 128:(j + 1) * 128],
                                        drhsN[:, g * 512:(g + 1) * 512],
                                        start=True, stop=one)
                    mm_s = P.do("tensor", dmm1,
                                waits=[st["psfree"][pp], e_dk, e_dv],
                                inc="t" if lyr == 1 else None)
                    if lyr == 0:
                        def dmm2(e, j=j, g=g):
                            return e.matmul(self.ps[j % 2][:, :],
                                            drb[:, j * 128:(j + 1) * 128],
                                            drhsR[:, g * 512:(g + 1) * 512],
                                            start=False, stop=True)
                        mm_s = P.do("tensor", dmm2, inc="t")
                    e_mk = P.do("vector",
                                lambda e, pp=pp, j=j: e.tensor_tensor(
                                    self.ps[pp][:, :], self.ps[pp][:, :],
                                    self.s_masks.ap()[:, j * 512:(j + 1) * 512],
                                    ALU.add),
                                waits=[mm_s], inc="v")
                    e_e = P.do("scalar",
                               lambda e, pp=pp: e.activation(
                                   expb[pp], self.ps[pp][:, :], AF.Exp,
                                   bias=self.cst(0.0), scale=1.0),
                               waits=[e_mk, st["expfree"][pp]], inc="a")
                    st["psfree"][pp] = e_e

                    def dmmpv(e, j=j, dvp=dvp):
                        return e.matmul(self.ps[2][0:dvp, :],
                                        bass.AP(self.s_work, WDV + j * dvp,
                                                [[32768, 128], [1, dvp]]),
                                        expb[j % 2], start=(j == 0), stop=(j == 3))
                    wpv = [e_e] + ([st["ps2free"]] if j == 0 else [])
                    mmv = P.do("tensor", dmmpv, waits=wpv, inc="t")
                    if lyr == 0:
                        def dmmz(e, j=j):
                            return e.matmul(self.ps[3][0:1, :], self.s_onesb[:, 0:1],
                                            expb[j % 2], start=(j == 0), stop=(j == 3))
                        mmz_e = P.do("tensor", dmmz,
                                     waits=[st["ps3free"]] if j == 0 else [], inc="t")
                        st["expfree"][pp] = mmz_e
                    else:
                        st["expfree"][pp] = mmv
                    mmv_last = mmv
                st["dg"] = mmv_last
                st["dvu"] = mmv_last
                st["drhs"] = mmv_last
                sg = g % 4
                e_ev = P.do("vector",
                            lambda e, sg=sg, dvp=dvp: e.tensor_copy(
                                ost[sg][0:dvp, :], self.ps[2][0:dvp, :]),
                            waits=[mmv_last, st["stg"][sg]], inc="v")
                st["ps2free"] = e_ev

                def scfn(e, g=g, sg=sg, dvp=dvp):
                    return e.indirect_dma_start(
                        self.d_oacc.ap(), IOA(IX[0:dvp, 116 + g:117 + g], 0),
                        ost[sg][0:dvp, :], None, compute_op=ALU.add)
                e_sc = P.do("gpsimd", scfn, waits=[e_ev, st["chain"]],
                            inc=sem("oac"), amt=16)
                st["chain"] = e_sc
                st["stg"][sg] = e_sc
                if lyr == 0:
                    zg = g % 2
                    e_ez = P.do("vector",
                                lambda e, zg=zg: e.tensor_copy(zst[zg][0:1, :],
                                                               self.ps[3][0:1, :]),
                                waits=[mmz_e, st["zstg"][zg]], inc="v")
                    st["ps3free"] = e_ez

                    def szfn(e, g=g, zg=zg):
                        return e.indirect_dma_start(
                            self.d_zacc.ap(), IOA(IX[0:2, 126 + g:127 + g], 0),
                            zst[zg], None, compute_op=ALU.add)
                    e_sz = P.do("gpsimd", szfn, waits=[e_ez, st["chain"]],
                                inc=sem("oac"), amt=16)
                    st["chain"] = e_sz
                    st["zstg"][zg] = e_sz

            e_of = P.dma("sync", o_fin.rearrange("p (a c) -> p a c", a=2),
                         self.d_oacc.ap().rearrange("(a p) c -> p a c", p=128),
                         waits=[st["chain"]], inc=sem("of"))
            st["ofin"] = e_of
            if lyr == 0:
                e_z0 = P.dma("sync", zf[0], self.d_zacc.ap()[0:1, :],
                             waits=[st["chain"]] + [st["zfin"]], inc=sem("of"))
                e_z1 = P.dma("sync", zf[1], self.d_zacc.ap()[1:2, :],
                             waits=[], inc=sem("of"))
                st["zfin"] = e_z1
                e_zr = P.do("vector", lambda e: e.reciprocal(
                    zr, F[0:1, 4096:5120]), waits=[e_z1, getattr(self, "_zr_last", None)],
                    inc="v")
            last_mult = None
            for g in range(2):
                if lyr == 1:
                    e_zr = P.do("vector",
                                lambda e, g=g: e.reciprocal(
                                    zr[:, g * 512:(g + 1) * 512],
                                    o_fin[64:65, g * 512:(g + 1) * 512]),
                                waits=[e_of, getattr(self, "_zr_last", None)], inc="v")
                ev_b, psb = self.bcast_row(zr[:, g * 512:(g + 1) * 512], [e_zr], 4 + g)
                self._zr_last = ev_b
                e_bs = P.do("vector", lambda e, psb=psb: e.tensor_copy(bsb, psb[:]),
                            waits=[ev_b, last_mult], inc="v")
                e_mu = P.do("vector",
                            lambda e, g=g, dv=dv: e.tensor_mul(
                                obuf[g][0:dv, :], o_fin[0:dv, g * 512:(g + 1) * 512],
                                bsb[0:dv, :]),
                            waits=[e_bs, st["ob"][g], e_of], inc="v")
                last_mult = e_mu
                e_st = P.dma("sync", self.d_o.ap()[h * dv:(h + 1) * dv,
                                                   g * 512:(g + 1) * 512],
                             obuf[g][0:dv, :], waits=[e_mu], inc=sem(f"ob{g}"))
                st["ob"][g] = e_st
                o_evs.append(e_st)
            st["ofin"] = last_mult
        self.ev_attn = o_evs

    def stage_wofold(self, lyr, waits):
        P = self.P
        W = self.s_work.ap()
        A = self.s_aux.ap()
        o_sb = lambda k, c0, n: W[:, k * 1024 + c0: k * 1024 + c0 + n]
        g_sb = lambda k, c0, n: W[:, 16384 + k * 1024 + c0: 16384 + k * 1024 + c0 + n]
        sgst = [A[:, i * 512:(i + 1) * 512] for i in range(2)]
        tst = [A[:, 1024 + i * 128: 1024 + (i + 1) * 128] for i in range(2)]
        wname = "L0_attn_wo" if lyr == 0 else "L1_attn_Wo"
        ld_evs = []
        for k in range(16):
            ld_evs.append(P.dma("sync", o_sb(k, 0, 1024),
                                self.d_o.ap()[k * 128:(k + 1) * 128, :],
                                waits=list(waits) if k == 0 else [], inc=f"wol{lyr}"))
            ld_evs.append(P.dma("sync", g_sb(k, 0, 1024),
                                self.d_g.ap()[k * 128:(k + 1) * 128, :],
                                waits=[], inc=f"wol{lyr}"))
        st = dict(sg=[None, None], pst=[None, None], ts=[None, None], sgd=[])

        def evac_wo(m, tg, bi, ps, ev_mm):
            b = (m + tg) % 2
            e_sg = P.do("vector",
                        lambda e, m=m, tg=tg, b=b: e.tensor_mul(
                            sgst[b], ps[:], g_sb(m, tg * 512, 512)),
                        waits=[ev_mm, st["sg"][b]], inc="v")
            evs = []
            for i in range(4):
                pb = i % 2
                et = P.do("tensor",
                          lambda e, i=i, pb=pb, b=b: e.transpose(
                              self.pst[pb][:, 0:128],
                              sgst[b][:, i * 128:(i + 1) * 128], self.s_identb[:]),
                          waits=[e_sg, st["pst"][pb]], inc="t")
                ev = P.do("vector",
                          lambda e, pb=pb, i=i: e.tensor_copy(
                              tst[pb], self.pst[pb][:, 0:128]),
                          waits=[et, st["ts"][pb]], inc="v")
                ed = P.dma("sync",
                           self.d_sg.ap()[(tg * 4 + i) * 128:(tg * 4 + i + 1) * 128,
                                          m * 128:(m + 1) * 128],
                           tst[pb], waits=[ev], inc=f"wos{pb}{lyr}")
                st["pst"][pb] = ev
                st["ts"][pb] = ed
                st["sgd"].append(ed)
                evs.append(ed)
            st["sg"][b] = evs[-1]
            return [e_sg]

        if lyr == 0:
            self.proj_fm(self.w[wname], D, D, evac_wo, ld_evs[-2:], wst_off=0,
                         rhs_fn=lambda k, c0, n: o_sb(k, c0, n))
        else:
            self.proj_fm_T(self.d_wbf[3], D, D, evac_wo,
                           ld_evs[-2:] + [self.ev_wbf3], wst_off=0,
                           rhs_fn=lambda k, c0, n: o_sb(k, c0, n))
        src = self.x_in if lyr == 0 else self.d_x1
        dst = self.d_x1 if lyr == 0 else self.y_out
        f = self.s_f32.ap()
        xrow = [f[:, 0:2048], f[:, 2048:4096]]
        sgrow = [A[:, 2048 + i * 2048: 2048 + (i + 1) * 2048] for i in range(2)]
        sgf = f[:, 4096:6144]
        fold_evs = []
        prev_ld = [None, None]
        prev_add = None
        sg_fence = st["sgd"][-2:]
        for tt in range(8):
            b = tt % 2
            e_x = P.dma("sync", xrow[b], src.ap()[tt * 128:(tt + 1) * 128, :],
                        waits=[prev_ld[b]], inc=f"fxl{b}{lyr}")
            e_s = P.dma("sync", sgrow[b], self.d_sg.ap()[tt * 128:(tt + 1) * 128, :],
                        waits=sg_fence if tt < 2 else [prev_add], inc=f"fsl{b}{lyr}")
            e_w = P.do("vector", lambda e, b=b: e.tensor_copy(sgf, sgrow[b]),
                       waits=[e_s, prev_add], inc="v")
            e_a = P.do("vector", lambda e, b=b: e.tensor_add(xrow[b], xrow[b], sgf),
                       waits=[e_w, e_x], inc="v")
            prev_add = e_a
            e_st = P.dma("sync", dst.ap()[tt * 128:(tt + 1) * 128, :], xrow[b],
                         waits=[e_a], inc=f"fst{b}{lyr}")
            prev_ld[b] = e_st
            fold_evs.append(e_st)
        self.ev_fold = fold_evs

    def proj_fm_T(self, wbf_dram, n_in, n_out, evac_fn, waits, wst_off=0,
                  rhs_fn=None, tgs=(0, 1)):
        P = self.P
        if rhs_fn is None:
            rhs_fn = self.xn
        KCH = n_in // 128
        MCH = n_out // 128
        slab_elems = KCH * 128
        slab_free = [None, None]
        bank_free = [[], [], [], []]
        out_evs = []
        for m in range(MCH):
            sb = m % 2
            slab = self.s_wst.ap()[:, wst_off + sb * slab_elems:
                                   wst_off + (sb + 1) * slab_elems]
            slab3 = slab.rearrange("p (k c) -> p k c", k=KCH)
            slab_ev = None
            for k in range(KCH):
                slab_ev = P.do(
                    "scalar",
                    lambda e, m=m, k=k, slab3=slab3: e.dma_start_transpose(
                        slab3[:, k, :],
                        wbf_dram.ap()[m * 128:(m + 1) * 128, k * 128:(k + 1) * 128]),
                    waits=([slab_free[sb]] + list(waits)) if k == 0 else [],
                    inc=f"wt{sb}", amt=16)
            for ti, tg in enumerate(tgs):
                bi = (len(tgs) * m + ti) % 4
                mm_ev = None
                for k in range(KCH):
                    last = (k == KCH - 1)

                    def mmfn(e, k=k, bi=bi, slab3=slab3, tg=tg, last=last):
                        return e.matmul(self.ps[bi][:, :], slab3[:, k, :],
                                        rhs_fn(k, tg * 512, 512),
                                        start=(k == 0), stop=last)
                    wts = ([slab_ev] + bank_free[bi] + list(waits)) if k == 0 else []
                    mm_ev = P.do("tensor", mmfn, waits=wts, inc="t" if last else None)
                evs = evac_fn(m, tg, bi, self.ps[bi], mm_ev)
                bank_free[bi] = list(evs) if evs else [mm_ev]
                out_evs.extend(evs)
                if ti == len(tgs) - 1:
                    slab_free[sb] = mm_ev
        return out_evs

    def gather_one(self, name, din, dout, kind, op, outrows, waits):
        P = self.P

        def ccfn(e, din=din, dout=dout, kind=kind, op=op, outrows=outrows):
            out_ap = dout.ap() if outrows is None else dout.ap()[0:outrows, :]
            return e.collective_compute(kind, op, replica_groups=[list(range(NC))],
                                        ins=[din.ap().opt()], outs=[out_ap.opt()])
        return P.do("gpsimd", ccfn, waits=waits, inc=f"cc{name}")

    def stage_kvf1(self, waits):
        P = self.P
        self._regs = getattr(self, "_regs", {})
        A = self.s_aux.ap()
        Fb = self.s_f32.ap()
        sm = self.s_small.ap()
        base_w = list(waits)
        wb_evs = []
        for i, nm in enumerate(("L1_attn_Wq", "L1_attn_Wk", "L1_attn_Wv", "L1_attn_Wo")):
            wb_evs.append(P.dma("gpsimd", self.d_wbf[i].ap(), self.w[nm].ap(),
                                waits=base_w if i == 0 else [], inc=f"wbf{i}"))
        wf_sb = A[0:32, 2048:4096]
        wfT = A[:, 4096:4608]
        e_wf = P.dma("gpsimd", wf_sb, self.w["L1_attn_Wf"].ap(), waits=base_w, inc="wff")
        tr_ev = [None, None]
        for k in range(16):
            pb = k % 2
            et = P.do("tensor",
                      lambda e, k=k, pb=pb: e.transpose(
                          self.pst[pb][:, 0:32], wf_sb[:, k * 128:(k + 1) * 128],
                          self.s_identb[0:32, 0:32]),
                      waits=[e_wf, tr_ev[pb]] + base_w, inc="t")
            tr_ev[pb] = P.do("vector",
                             lambda e, k=k, pb=pb: e.tensor_copy(
                                 wfT[:, k * 32:(k + 1) * 32], self.pst[pb][:, 0:32]),
                             waits=[et], inc="v")
        f_own = self.s_stat.ap()[0:32, 1024:2048]
        u = sm[0:32, 22:23]
        fin_evs = []
        f_read = []
        for tg in range(2):
            mm = None
            for k in range(16):
                def mf(e, k=k, tg=tg):
                    return e.matmul(self.ps[4][0:32, :], wfT[:, k * 32:(k + 1) * 32],
                                    self.xn(k, tg * 512, 512),
                                    start=(k == 0), stop=(k == 15))
                mm = P.do("tensor", mf,
                          waits=(tr_ev + base_w + f_read) if k == 0 else [],
                          inc="t" if k == 15 else None)
            uu = Fb[0:32, 8192 + tg * 512: 8704 + tg * 512]
            aa = Fb[0:32, 9216 + tg * 512: 9728 + tg * 512]
            e1 = P.do("vector", lambda e, uu=uu: e.tensor_scalar_add(
                uu, self.ps[4][0:32, :], self.s_bf[:]),
                waits=[mm], inc="v")
            f_read.append(e1)
            e2 = P.do("scalar", lambda e, uu=uu, aa=aa: e.activation(
                aa, uu, AF.Abs, bias=0.0, scale=1.0), waits=[e1], inc="a")
            e3 = P.do("scalar", lambda e, aa=aa: e.activation(
                aa, aa, AF.Exp, bias=0.0, scale=-1.0), waits=[e2], inc="a")
            e4 = P.do("scalar", lambda e, aa=aa: e.activation(
                aa, aa, AF.Ln, bias=self.cst(1.0)[0:32, :], scale=1.0),
                waits=[e3], inc="a")
            e5 = P.do("vector", lambda e, uu=uu: e.tensor_single_scalar(
                uu, uu, 0.0, ALU.min), waits=[e4], inc="v")
            e6 = P.do("vector", lambda e, uu=uu, aa=aa, tg=tg: e.tensor_sub(
                f_own[:, tg * 512:(tg + 1) * 512], uu, aa), waits=[e5], inc="v")
            fin_evs.append(P.dma("sync", self.d_fin.ap()[:, tg * 512:(tg + 1) * 512],
                                 f_own[:, tg * 512:(tg + 1) * 512],
                                 waits=[e6], inc="fing"))
        ev_gf = self.gather_one("f1", self.d_fin, self.d_fout, "AllGather",
                                ALU.bypass, None, fin_evs)
        qst = [A[:, i * 512:(i + 1) * 512] for i in range(2)]
        qsq = [A[:, 1024 + i * 512: 1536 + i * 512] for i in range(2)]
        stq = dict(free={}, nmm=[None, None], dma=[])

        def evac_q(m, tg, bi, ps, ev_mm):
            b = (m + tg) % 2
            e1 = P.do("vector", lambda e, b=b: e.tensor_scalar_mul(qst[b], ps[:], SC1),
                      waits=[ev_mm, stq["free"].get(("q", b))], inc="v")
            ed = P.dma("sync", self.d_q.ap()[m * 128:(m + 1) * 128,
                                             tg * 512: tg * 512 + 512],
                       qst[b], waits=[e1], inc=f"qst{b}")
            e2 = P.do("scalar", lambda e, b=b: e.activation(
                qsq[b], qst[b], AF.Square, bias=0.0, scale=1.0),
                waits=[e1, stq["free"].get(("s", b))], inc="a")

            def nmm(e, m=m, tg=tg, b=b):
                return e.matmul(self.ps[4 + tg][0:32, :],
                                self.s_sel1.ap()[:, m * 32:(m + 1) * 32],
                                qsq[b], start=(m == 0), stop=(m == 15))
            en = P.do("tensor", nmm,
                      waits=[e2] + ((f_read + [stq["nmm"][tg]]) if m == 0 else []),
                      inc="t")
            stq["nmm"][tg] = en
            stq["free"][("q", b)] = ed
            stq["free"][("s", b)] = en
            stq["dma"].append(ed)
            return [ed, en]

        evs_q = self.proj_fm_T(self.d_wbf[0], D, D, evac_q, wb_evs[0:1] + base_w,
                               wst_off=0)
        qm = [sm[0:32, 24:25], sm[0:32, 25:26]]
        ev_qm = []
        for tg in range(2):
            ev = P.do("vector", lambda e, tg=tg: e.tensor_reduce(
                qm[tg], self.ps[4 + tg][0:32, :], AX.X, ALU.max),
                waits=[stq["nmm"][tg]], inc="v")
            ev_qm.append(ev)
        kst = [A[:, i * 512:(i + 1) * 512] for i in range(2)]
        ksq = [A[:, 1024 + i * 512: 1536 + i * 512] for i in range(2)]
        stk = dict(free={}, nmm=[None, None], dma=[])

        def evac_k(m, tg, bi, ps, ev_mm):
            b = (m + tg) % 2
            e1 = P.do("vector", lambda e, b=b: e.tensor_copy(kst[b], ps[:]),
                      waits=[ev_mm, stk["free"].get(("q", b)),
                             evs_q[-1] if m == 0 and tg == 0 else None,
                             stq["free"].get(("q", b)) if m == 0 else None],
                      inc="v")
            ed1 = P.dma("sync", self.d_kin.ap()[2 * m * 67: 2 * m * 67 + 64,
                                                tg * 512: tg * 512 + 512],
                        kst[b][0:64, :], waits=[e1], inc=f"ksb{b}")
            ed2 = P.dma("sync", self.d_kin.ap()[(2 * m + 1) * 67: (2 * m + 1) * 67 + 64,
                                                tg * 512: tg * 512 + 512],
                        kst[b][64:128, :], waits=[e1], inc=f"ksb{b}")
            e2 = P.do("scalar", lambda e, b=b: e.activation(
                ksq[b], kst[b], AF.Square, bias=0.0, scale=1.0),
                waits=[e1, stk["free"].get(("s", b)),
                       ev_qm[0] if m == 0 and tg == 0 else None,
                       ev_qm[1] if m == 0 and tg == 0 else None,
                       stq["free"].get(("s", b)) if m == 0 else None], inc="a")

            def nmm(e, m=m, tg=tg, b=b):
                return e.matmul(self.ps[4 + tg][0:32, :],
                                self.s_sel1.ap()[:, m * 32:(m + 1) * 32],
                                ksq[b], start=(m == 0), stop=(m == 15))
            en = P.do("tensor", nmm,
                      waits=[e2] + (ev_qm + [stk["nmm"][tg]] if m == 0 else []),
                      inc="t")
            stk["nmm"][tg] = en
            stk["free"][("q", b)] = ed2
            stk["free"][("s", b)] = en
            stk["dma"].extend([ed1, ed2])
            return [ed2, en]

        evs_k = self.proj_fm_T(self.d_wbf[1], D, D, evac_k, wb_evs[1:2], wst_off=4096)
        km = [sm[0:32, 26:27], sm[0:32, 27:28]]
        e1 = P.do("vector", lambda e: e.tensor_reduce(km[0], self.ps[4][0:32, :],
                                                      AX.X, ALU.max),
                  waits=[stk["nmm"][0]], inc="v")
        e2 = P.do("vector", lambda e: e.tensor_reduce(km[1], self.ps[5][0:32, :],
                                                      AX.X, ALU.max),
                  waits=[stk["nmm"][1]], inc="v")
        e3 = P.do("vector", lambda e: e.tensor_max(sm[0:32, 28:29], km[0], km[1]),
                  waits=[e1, e2], inc="v")
        en_dma = P.dma("sync", self.d_nin.ap(), sm[0:32, 28:29], waits=[e3], inc="ning")
        ev_gn = self.gather_one("n1", self.d_nin, self.d_nout, "AllReduce",
                                ALU.max, None, [en_dma])
        vst = [A[:, i * 512:(i + 1) * 512] for i in range(2)]
        vt = [A[:, 1024 + i * 512: 1536 + i * 512] for i in range(2)]
        stv = dict(free={}, pst=[None, None], vt=[None, None], dma=[])

        def evac_v(m, tg, bi, ps, ev_mm):
            b = (m + tg) % 2
            e1 = P.do("vector", lambda e, b=b: e.tensor_copy(vst[b], ps[:]),
                      waits=[ev_mm, stv["free"].get(b),
                             evs_k[-1] if m == 0 and tg == 0 else None,
                             stk["free"].get(("q", b)) if m == 0 else None,
                             stk["free"].get(("s", b)) if m == 0 else None],
                      inc="v")
            evs = []
            for i in range(4):
                pb = i % 2
                et = P.do("tensor",
                          lambda e, i=i, pb=pb, b=b: e.transpose(
                              self.pst[pb][:, 0:128],
                              vst[b][:, i * 128:(i + 1) * 128], self.s_identb[:]),
                          waits=[e1, stv["pst"][pb]], inc="t")
                ev = P.do("vector",
                          lambda e, i=i, pb=pb, b=b: e.tensor_copy(
                              vt[b][:, i * 128:(i + 1) * 128], self.pst[pb][:, 0:128]),
                          waits=[et, stv["vt"][b]], inc="v")
                stv["pst"][pb] = ev
                evs.append(ev)
            ed = P.dma("sync",
                       self.d_vin.ap()[tg * 512: tg * 512 + 512,
                                       m * 128:(m + 1) * 128]
                       .rearrange("(a p) c -> p a c", p=128),
                       vt[b].rearrange("p (a c) -> p a c", a=4),
                       waits=evs, inc=f"vt{b}")
            stv["free"][b] = evs[-1]
            stv["vt"][b] = ed
            stv["dma"].append(ed)
            return [ed]

        evs_v = self.proj_fm_T(self.d_wbf[2], D, D, evac_v, wb_evs[2:3], wst_off=8192)
        sF = Fb[0:32, 0:8192]
        fa_evs = []
        for C in range(16):
            r, hf = chunk_rank(C), chunk_half(C)
            fa_evs.append(P.dma("sync", sF[:, C * 512:(C + 1) * 512],
                                self.d_fout.ap()[r * 32:(r + 1) * 32,
                                                 hf * 512:(hf + 1) * 512],
                                waits=[ev_gf] if C == 0 else [], inc="fas"))
        stmp = Fb[0:32, 8192:10240]
        carry = None
        ev_sc = fa_evs[-1]
        for seg in range(4):
            def scn(e, seg=seg, carry=carry):
                return e.tensor_tensor_scan(
                    stmp, sF[:, seg * 2048:(seg + 1) * 2048],
                    sF[:, seg * 2048:(seg + 1) * 2048],
                    0.0 if seg == 0 else carry, ALU.add, ALU.bypass)
            ev_s1 = P.do("vector", scn, waits=[ev_sc] + (fa_evs if seg == 0 else []),
                         inc="v")
            ev_sc = P.do("vector",
                         lambda e, seg=seg: e.tensor_copy(
                             sF[:, seg * 2048:(seg + 1) * 2048], stmp),
                         waits=[ev_s1], inc="v")
            carry = sF[0:32, (seg + 1) * 2048 - 1: (seg + 1) * 2048]
        ev_ld = P.dma("sync", sm[0:32, 29:30], self.d_nout.ap(), waits=[ev_gn],
                      inc="nldg")
        e_ks = P.do("scalar", lambda e: e.activation(sm[0:32, 29:30], sm[0:32, 29:30],
                                                     AF.Sqrt, bias=0.0, scale=1.0),
                    waits=[ev_ld], inc="a")
        e_qx = P.do("vector", lambda e: e.tensor_max(sm[0:32, 30:31], qm[0], qm[1]),
                    waits=ev_qm, inc="v")
        e_qs = P.do("scalar", lambda e: e.activation(sm[0:32, 30:31], sm[0:32, 30:31],
                                                     AF.Sqrt, bias=0.0, scale=1.0),
                    waits=[e_qx], inc="a")
        e_c1 = P.do("vector", lambda e: e.tensor_mul(sm[0:32, 31:32], sm[0:32, 29:30],
                                                     sm[0:32, 30:31]),
                    waits=[e_ks, e_qs], inc="v")
        F_own = Fb[0:32, 8192:9216]
        ev_fd = P.dma("sync", self.d_F.ap(), sF, waits=[ev_sc], inc="fxo")

        def fx(e, g):
            r0 = self._reg(e, "rf0")
            e.load(r0, self.s_tab.ap()[0:1, TFW + g:TFW + g + 1])
            return e.dma_start(F_own[:, g * 512:(g + 1) * 512],
                               bass.AP(self.d_F, r0, [[8192, 32], [1, 512]]))
        ev_f1 = P.do("sync", lambda e: fx(e, 0), waits=[ev_fd], inc="fxo", amt=16)
        ev_f2 = P.do("sync", lambda e: fx(e, 1), waits=[ev_fd], inc="fxo", amt=16)
        bsp = A[0:96, 2048:3072]
        nsp = A[0:96, 3072:4096]
        t1 = Fb[0:32, 9216:10240]
        t2 = Fb[0:96, 6144:7168]
        evs_sp = []
        for which, dstbuf in (("b", bsp), ("n", nsp)):
            if which == "b":
                e0 = P.do("vector", lambda e: e.tensor_scalar(
                    t1, F_own, sm[0:32, 31:32], None, ALU.subtract, ALU.bypass),
                    waits=[ev_f1, ev_f2, e_c1], inc="v")
            else:
                e0 = P.do("vector", lambda e: e.tensor_scalar_mul(t1, F_own, -1.0),
                          waits=[ev_f1, ev_f2, evs_sp[-1]], inc="v")
            eh = P.do("vector", lambda e, d=dstbuf: e.tensor_copy(d[0:32, :], t1),
                      waits=[e0], inc="v")
            ew = P.do("vector", lambda e, d=dstbuf: e.tensor_copy(t2[0:32, :], d[0:32, :]),
                      waits=[eh], inc="v")
            er = P.do("vector", lambda e: e.tensor_sub(t1, t1, t2[0:32, :]),
                      waits=[ew], inc="v")
            em = P.do("vector", lambda e, d=dstbuf: e.tensor_copy(d[32:64, :], t1),
                      waits=[er], inc="v")
            ew2 = P.do("vector", lambda e, d=dstbuf: e.tensor_copy(t2[0:32, :], d[32:64, :]),
                       waits=[em], inc="v")
            er2 = P.do("vector", lambda e: e.tensor_sub(t1, t1, t2[0:32, :]),
                       waits=[ew2], inc="v")
            el = P.do("vector", lambda e, d=dstbuf: e.tensor_copy(d[64:96, :], t1),
                      waits=[er2], inc="v")
            evs_sp.append(el)
        ev_bd = P.dma("sync", self.d_q.ap()[2048:2144, :], bsp, waits=[evs_sp[0]],
                      inc="bdg")
        nf_evs = []
        for h in range(H1):
            for j in range(3):
                nf_evs.append(P.dma(
                    "sync", self.d_kin.ap()[h * 67 + 64 + j: h * 67 + 65 + j, :],
                    self.s_aux.ap()[32 * j + h: 32 * j + h + 1, 3072:4096],
                    waits=[evs_sp[1]] if h == 0 and j == 0 else [], inc="nfk"))
        ev_gk = self.gather_one("k1", self.d_kin, self.d_kout, "AllGather",
                                ALU.bypass, 8 * 2240,
                                stk["dma"][-2:] + nf_evs[-1:] + [nf_evs[0]])
        ev_gv = self.gather_one("v1", self.d_vin, self.d_vout, "AllGather",
                                ALU.bypass, 8192, stv["dma"][-2:])
        self.ev_gather1 = dict(k=ev_gk, v=ev_gv, f=ev_gf, n=ev_gn)
        self.ev_wbf3 = wb_evs[3]
        self.ev_kvf1 = dict(q=stq["dma"], bd=ev_bd, c1=e_c1)
        self.ev_kvf1_all = stq["dma"][-2:] + [ev_bd, evs_sp[1], ev_sc]

    def gathers(self, waits, tag):
        P = self.P
        evs = {}
        specs = [("k", self.d_kin, self.d_kout, "AllGather", ALU.bypass, 8 * 2240),
                 ("v", self.d_vin, self.d_vout, "AllGather", ALU.bypass, 8192),
                 ("r", self.d_rin, self.d_rout, "AllGather", ALU.bypass, 512),
                 ("n", self.d_nin, self.d_nout, "AllReduce", ALU.max, None)]
        for name, din, dout, kind, op, lim in specs:
            sem = f"cc{name}{tag}"
            out_ap = dout.ap() if lim is None else dout.ap()[0:lim, :]
            def ccfn(e, din=din, out_ap=out_ap, kind=kind, op=op):
                return e.collective_compute(
                    kind, op, replica_groups=[list(range(NC))],
                    ins=[din.ap().opt()], outs=[out_ap.opt()])
            evs[name] = P.do("gpsimd", ccfn, waits=waits, inc=sem)
        return evs


def build_module(dbg=(), stages=("ln0",)):
    mk = MK(dbg)
    mk.prolog()
    if "ln0" in stages:
        mk.stage_ln(0, ntiles=1 if "ln1tile" in dbg else 8)
    if "q0" in stages and "kv0" not in stages:
        raise RuntimeError("q0 needs kv0")
    if "kv0" in stages:
        mk.skip_gather = "nogather" in mk.dbg
        mk.kv0_half = "kv0half" in mk.dbg
        if "kv0lvl0" in mk.dbg:
            mk.kv0_level = 0
        elif "kv0lvl1" in mk.dbg:
            mk.kv0_level = 1
        elif "kv0lvl2" in mk.dbg:
            mk.kv0_level = 2
        mk.stage_kv0()
        if "kv0" in mk.dbg and mk.skip_gather:
            P = mk.P
            wl = mk._kv0_own_done
            if getattr(mk, "kv0_level", 3) < 2:
                o = mk.dbg_add("ckvraw", [128, 4096], BF16)
                mk.P.dma("gpsimd", o.ap(), mk.s_work.ap()[:, 0:4096],
                         waits=mk._kv0_own_done, inc="g")
                dumps = []
            else:
                dumps = [("rin", mk.d_rin, [64, 1024])]
            if not mk.kv0_half:
                dumps += [("kin", mk.d_kin, [2048, 1024]),
                          ("vin", mk.d_vin, [1024, 2048])]
            for nm, t, sh in dumps:
                o = mk.dbg_add(nm, sh, BF16)
                P.dma("gpsimd", o.ap(), t.ap(), waits=wl, inc="g")
            if not mk.kv0_half:
                o = mk.dbg_add("nout", [32, 1], F32)
                P.dma("gpsimd", o.ap(), mk.d_nin.ap(), waits=wl, inc="g")
    if "q0" in stages:
        mk.stage_q0()
        if "q0" in mk.dbg:
            o = mk.dbg_add("q", [3072, 1024], BF16)
            mk.P.dma("gpsimd", o.ap(), mk.d_q.ap(), waits=mk.ev_q0_all, inc="g")
            o = mk.dbg_add("crow", [16, 1024], BF16)
            mk.P.dma("gpsimd", o.ap(), mk.s_aux.ap()[0:16, 4096:5120],
                     waits=mk.ev_q0_all, inc="g")
    if "ffn0" in stages:
        mk.stage_ffn(0, mk.ev_q0_all)
        if "ffn0" in mk.dbg:
            o = mk.dbg_add("g", [2048, 1024], BF16)
            mk.P.dma("gpsimd", o.ap(), mk.d_g.ap(), waits=mk.ev_ffn, inc="g")
    if "attn0" in stages:
        g = mk.ev_gather0
        vre = mk.stage_vreorder(0, [g["v"], mk.ev_prolog_last]
                                + mk.ev_ffn[-2:] + mk.ev_q0_all[-2:])
        vjoin = mk.P.do("vector", lambda e: e.engine_nop(),
                        waits=[vre[-1], vre[-2]], inc="v")
        qsl = mk.stage_qslots(0, mk.ev_q0_all)
        mk.stage_attn(0, mk.ev_ffn + mk.ev_q0_all[-2:],
                      kw=g["k"], rw0=g["r"], vw=vjoin, qw=qsl[-1])
        if "attn0" in mk.dbg:
            o = mk.dbg_add("o0", [2048, 1024], BF16)
            mk.P.dma("gpsimd", o.ap(), mk.d_o.ap(), waits=mk.ev_attn, inc="g")
    if "wofold0" in stages:
        mk.stage_wofold(0, mk.ev_attn[-2:] + mk.ev_ffn[-2:])
        if "wofold0" in mk.dbg:
            o = mk.dbg_add("x1", [1024, 2048], F32)
            mk.P.dma("gpsimd", o.ap(), mk.d_x1.ap(), waits=mk.ev_fold, inc="g")
    if "l1" in stages:
        P = mk.P
        mk.ev_fold_done = mk.ev_fold[-2:]
        mk.stage_ln(1)
        mk.stage_kvf1(mk.ev_ln + mk.ev_fold[-2:])
        mk.stage_ffn(1, mk.ev_kvf1_all + mk.ev_ln)
        g1 = mk.ev_gather1
        vre1 = mk.stage_vreorder(1, [g1["v"], mk.ev_prolog_last] + mk.ev_ffn[-2:]
                                 + mk.ev_kvf1_all)
        vjoin1 = P.do("vector", lambda e: e.engine_nop(),
                      waits=[vre1[-1], vre1[-2]], inc="v")
        qsl1 = mk.stage_qslots(1, mk.ev_kvf1_all)
        mk.stage_attn(1, mk.ev_ffn + mk.ev_kvf1_all,
                      kw=g1["k"], rw0=None, vw=vjoin1, qw=qsl1[-1])
        mk.stage_wofold(1, mk.ev_attn[-2:] + mk.ev_ffn[-2:])
        if "y" in mk.dbg:
            o = mk.dbg_add("y", [1024, 2048], F32)
            mk.P.dma("gpsimd", o.ap(), mk.y_out.ap(), waits=mk.ev_fold, inc="g")
    if "kv0" in stages:
        if "kv0" in mk.dbg and not mk.skip_gather:
            g = mk.ev_gather0
            P = mk.P
            for nm, t, sh in [("kin", mk.d_kin, [2048, 1024]),
                              ("vin", mk.d_vin, [1024, 2048]),
                              ("rin", mk.d_rin, [64, 1024])]:
                o = mk.dbg_add(nm, sh, BF16)
                P.dma("gpsimd", o.ap(), t.ap(), waits=[g["k"], g["v"], g["r"]], inc="g")
            o = mk.dbg_add("nout", [32, 1], F32)
            P.dma("gpsimd", o.ap(), mk.d_nout.ap(), waits=[g["n"]], inc="g")
            o = mk.dbg_add("kout0", [2048, 1024], BF16)
            P.dma("gpsimd", o.ap(), mk.d_kout.ap()[0:2048, :], waits=[g["k"]], inc="g")
        if "xn0" in mk.dbg:
            mk.dbg_dump_sbuf("xn0", mk.s_xn.ap(), [128, 16 * 1024], BF16,
                             waits=mk.ev_ln)
    P = mk.P
    fin = [(s, P.count[s]) for s in list(P.count) if P.count[s] > 0]

    def wait_all(e):
        for s, v in fin:
            e.wait_ge(P.sems[s], v)
        return None
    if "noemit" not in mk.dbg:
        P.do("sync", wait_all)
        mk.P.emit()
    return mk


def shard_x(x):
    xs = []
    for c in range(NC):
        gA, gB = own_chunks(c)
        xs.append(np.ascontiguousarray(
            np.concatenate([x[0, gA * 512:(gA + 1) * 512],
                            x[0, gB * 512:(gB + 1) * 512]], axis=0)))
    return xs


def unshard_y(ys):
    out = np.empty((1, S, D), np.float32)
    for c in range(NC):
        gA, gB = own_chunks(c)
        out[0, gA * 512:(gA + 1) * 512] = ys[c][0:512]
        out[0, gB * 512:(gB + 1) * 512] = ys[c][512:1024]
    return out


def make_inputs(core_inputs, **inputs):
    consts = host_consts()
    in_maps = []
    xs = shard_x(inputs["L0_attn_hidden_states"])
    for c in range(NC):
        tab, mcol = host_tables(c)
        m = dict(x_in=xs[c], tab=tab, mcol=mcol, sidx=host_idx(c),
                 masks=consts["masks"], identb=consts["identb"],
                 identf=consts["identf"], sel0=consts["sel0"],
                 sel1=consts["sel1"], cs=host_cossin(c))
        for n, v in inputs.items():
            if n in ("L0_attn_hidden_states", "L0_attn_gq_a", "L0_attn_gkv_a"):
                continue
            vv = np.asarray(v, np.float32)
            if vv.ndim == 1:
                vv = vv.reshape(-1, 1)
            m[n] = np.ascontiguousarray(vv)
        m["gq_cols"] = np.ascontiguousarray(
            np.asarray(inputs["L0_attn_gq_a"], np.float32).reshape(12, 128).T)
        m["gkv_cols"] = np.ascontiguousarray(
            np.asarray(inputs["L0_attn_gkv_a"], np.float32).reshape(4, 128).T)
        in_maps.append(m)
    return in_maps


ALL_STAGES = ("ln0", "kv0", "q0", "ffn0", "attn0", "wofold0", "l1")


def kernel(**inputs):
    mk = build_module(stages=ALL_STAGES)
    in_maps = make_inputs(None, **inputs)
    res = run_bass_kernel_spmd(mk.nc, in_maps, core_ids=list(range(NC)))
    ys = [res.results[c]["y_out"] for c in range(NC)]
    return unshard_y(ys)
